# version 10
# baseline (speedup 1.0000x reference)
import numpy as np
import ml_dtypes
import concourse.bass as bass
import concourse.mybir as mybir
from concourse.bass_types import AP
from concourse.bass_utils import run_bass_kernel_spmd
from concourse.aot_env import allow_ici_all_to_all
from contextlib import ExitStack

F32 = mybir.dt.float32
BF16 = mybir.dt.bfloat16
I32 = mybir.dt.int32
I8 = mybir.dt.int8
AF = mybir.ActivationFunctionType
ALU = mybir.AluOpType

NCORE = 8
TOK = 1024
GT = 8192
HID = 2048
DL = 512
NST = 16
FFI = 8192
NE, FF = 8, 512
TB = 512
NB = GT // TB
RG = [list(range(NCORE))]
EPS = 1e-5


class Sem:
    def __init__(self, name):
        self.name, self.n, self.h = name, 0, None


class _Chain:
    def __init__(self, e):
        self.e = e

    def then_inc(self, sem, k=1):
        self.e[3].append((sem, k))
        sem.n += k
        return self


class Rec:
    def __init__(self, name):
        self.name, self.calls = name, []

    def wait(self, sem, tgt=None):
        t = sem.n if tgt is None else tgt
        if t > 0:
            self.calls.append(("wait_ge", (sem, t), {}, []))

    def op(self, meth, *a, **kw):
        e = [meth, a, kw, []]
        self.calls.append(e)
        return _Chain(e)

    def __getattr__(self, meth):
        if meth.startswith("_") or meth in ("name", "calls"):
            raise AttributeError(meth)

        def call(*a, **kw):
            return self.op(meth, *a, **kw)

        return call

    NO_DRAIN = {"wait_ge", "dma_start", "collective_compute", "memset"}

    def replay(self, eng, auto_drain=False):
        for meth, a, kw, incs in self.calls:
            if meth == "wait_ge":
                eng.wait_ge(a[0].h, a[1])
                continue
            inst = getattr(eng, meth)(*a, **kw)
            for sem, k in incs:
                inst.then_inc(sem.h, k)
            if auto_drain and meth not in Rec.NO_DRAIN:
                eng.drain()


class DStream:

    _all = []

    def __init__(self, name, nslots=1):
        self.name = name
        self.slots = [Sem(f"{name}{i}") for i in range(nslots)]
        DStream._all.append(self)

    def issue(self, rec, slot, out, in_, gate=None):
        if gate is not None and gate[0] is not None:
            rec.wait(gate[0], gate[1])
        rec.dma_start(out=out, in_=in_).then_inc(self.slots[slot], 16)
        return (self.slots[slot], self.slots[slot].n)

    def tgt(self, slot=0):
        return (self.slots[slot], self.slots[slot].n)


def splice_to(dst, tmp, state, upto):
    while state[0] < upto:
        dst.calls.append(tmp.calls[state[0]])
        state[0] += 1


def splice(dst, tmp, state, frac):
    total = state[1]
    want = min(total, int(np.ceil(total * frac)))
    while state[0] < want:
        dst.calls.append(tmp.calls[state[0]])
        state[0] += 1


def build_nc():
    nc = bass.Bass("TRN2", target_bir_lowering=False)

    def din(name, shape, dt=F32):
        return nc.dram_tensor(name, shape, dt, kind="ExternalInput").ap()

    x_t = din("x_t", [HID, TOK])
    ipw = din("ipw", [HID, 2 * DL])
    convw = din("convw", [128, 4, 4])
    convb = din("convb", [128, 4])
    alog = din("alog", [128, 4, NST])
    dparam = din("dparam", [128, 4])
    dtbi = din("dtbi", [128, 4])
    xpw = din("xpw", [128, 4, 160])
    dtw = din("dtw", [128, DL])
    opw = din("opw", [4096, HID])
    gw = din("gw", [HID, FFI], I32)
    uw = din("uw", [HID, FFI], I32)
    dwn = din("dwn", [FFI, HID], I32)
    gsc = din("gsc", [128, 64])
    usc = din("usc", [128, 64])
    dsc = din("dsc", [128, 16])
    wqk = din("wqk", [HID, HID])
    wv = din("wv", [HID, HID])
    wu = din("wu", [HID, HID])
    wo = din("wo", [HID, HID])
    bqk = din("bqk", [128, 16])
    bvr = din("bvr", [1, HID])
    bui = din("bui", [128, 16])
    boi = din("boi", [128, 16])
    lng = din("lng", [128, 16])
    lnb = din("lnb", [128, 16])
    rw = din("rw", [HID, NE])
    w1 = din("w1", [NE, HID, FF])
    w2 = din("w2", [NE, FF, HID])
    w3 = din("w3", [NE, HID, FF])
    idf = din("idf", [128, 128])

    out_t = nc.dram_tensor("out_t", [HID, TOK], F32, kind="ExternalOutput").ap()

    def dint(name, shape, dt):
        return nc.dram_tensor(name, shape, dt).ap()

    g_xn_in = dint("g_xn_in", [128, 16, TOK], BF16)
    g_xn_out = dint("g_xn_out", [NCORE, 128, 16, TOK], BF16)
    g_ssm_in = dint("g_ssm_in", [160, GT], F32)
    g_ssm_out = dint("g_ssm_out", [160, GT], F32)
    g_bc = dint("g_bc", [32, GT], BF16)
    g_y_in = dint("g_y_in", [NCORE, 128, 4, TOK], BF16)
    g_y_out = dint("g_y_out", [NCORE, 128, 4, TOK], BF16)
    g_kv_in = dint("g_kv_in", [128, HID], F32)
    g_kv_out = dint("g_kv_out", [128, HID], F32)
    g_wts = dint("g_wts", [NE, TOK], BF16)
    sp_delta = dint("sp_delta", [128, 4, GT], BF16)
    sp_u = dint("sp_u", [128, 4, GT], BF16)
    sp_gate = dint("sp_gate", [128, 4, GT], BF16)
    sp_h = dint("sp_h", [64, 128, TOK], BF16)
    sp_g0 = dint("sp_g0", [16, 128, TOK], BF16)
    sp_x1 = dint("sp_x1", [16, 128, TOK], F32)
    sp_u1 = dint("sp_u1", [16, 128, TOK], BF16)

    ctx = ExitStack()
    with ctx:
        AREN_A, AREN_B = 65536, 81920
        _arena = nc.alloc_sbuf_tensor("arena", [128, AREN_A + AREN_B], I8)
        _AB = nc.lookup_mloc(_arena).addr
        BB = AREN_A

        def at(name, shape, dt, off):
            return nc.alloc_sbuf_tensor_at(name, shape, dt, offset=_AB + off)

        ipw_sb = at("ipw_sb", [128, 16, 2 * DL], BF16, 0)
        xng = [at(f"xng{i}", [128, 16, 512], BF16, 32768 + i * 16384) for i in range(2)]
        sc_de = [at(f"sc_de{i}", [128, 4, TB], BF16, i * 4096) for i in range(2)]
        sc_u = [at(f"sc_u{i}", [128, 4, TB], BF16, 8192 + i * 4096) for i in range(2)]
        sc_ga = at("sc_ga", [128, 4, TB], BF16, 16384)
        sc_du = at("sc_du", [128, 4, TB], BF16, 20480)
        brep = [at(f"brep{i}", [128, 4, TB], BF16, 24576 + i * 4096) for i in range(2)]
        crep = [at(f"crep{i}", [128, 4, TB], BF16, 32768 + i * 4096) for i in range(2)]
        sc_hC = at("sc_hC", [128, 4, TB], BF16, 40960)
        sc_dA = [at(f"sc_dA{i}", [128, TB], BF16,
                    (45056 + i * 1024) if i < 2 else (53248 + (i - 2) * 1024))
                 for i in range(8)]
        sc_b = [at(f"sc_b{i}", [128, TB], BF16, 47104 + i * 1024) for i in range(2)]
        sc_h = [at(f"sc_h{i}", [128, TB], BF16,
                   (49152 + i * 1024) if i < 2 else (59392 + (i - 2) * 1024))
                for i in range(4)]
        sc_yf = [at(f"sc_yf{i}", [128, TB], BF16, 51200 + i * 1024) for i in range(2)]
        y_sb = at("y_sb", [128, 32, TOK], BF16, 0)
        qtok = at("qtok", [128, 8, 16, 128], BF16, 0)
        vtok = at("vtok", [128, 8, HID], BF16, 32768)
        he_sb = at("he_sb", [128, NE * 4, TOK], BF16, 0)
        ao_sb = at("ao_sb", [128, 16, TOK], BF16, 0)
        gated = at("gated", [128, 16, TOK], BF16, 32768)
        x1l = [at(f"x1l{i}", [128, TOK], F32, i * 4096) for i in range(2)]
        outst = [at(f"outst{i}", [128, TOK], F32, 8192 + i * 4096) for i in range(2)]

        xch = [at(f"xch{i}", [128, TOK], F32, BB + i * 4096) for i in range(2)]
        xsqf = [at(f"xsqf{i}", [128, TOK], F32, BB + 8192 + i * 4096) for i in range(2)]
        mean_rep = at("mean_rep", [128, TOK], F32, BB + 16384)
        rstd_rep = at("rstd_rep", [128, TOK], F32, BB + 20480)
        e2_rep = at("e2_rep", [128, TOK], F32, BB + 24576)
        xcbuf = at("xcbuf", [128, 4, 516], BF16, BB + 0)
        ubuf = [at(f"ubuf{i}", [128, 4, 512], BF16, BB + 8256 + i * 4096)
                for i in range(2)]
        gstg = [at(f"gstg{i}", [128, 512], BF16, BB + 48128 + i * 1024) for i in range(2)]
        bcst = [at(f"bcst{i}", [32, 512], F32, BB + 50176 + i * 2048) for i in range(2)]
        dstg = [at(f"dstg{i}", [128, 512], F32, BB + 54272 + i * 2048) for i in range(2)]
        dstg2 = [at(f"dstg2{i}", [128, 512], BF16, BB + 58368 + i * 1024) for i in range(2)]
        dtlr_sb = at("dtlr_sb", [128, GT], BF16, BB + 28672)
        gwb = [at(f"gwb{i}", [128, 16, 256], BF16, BB + 28672 + i * 8192) for i in range(2)]
        uwb = [at(f"uwb{i}", [128, 16, 256], BF16, BB + 45056 + i * 8192) for i in range(2)]
        f_tg = [at(f"f_tg{i}", [128, TOK], BF16, BB + 16384 + i * 2048) for i in range(2)]
        f_gv = [at(f"f_gv{i}", [128, TOK], BF16, BB + 20480 + i * 2048) for i in range(2)]
        f_h = [at(f"f_h{i}", [128, TOK], BF16, BB + 24576 + i * 2048) for i in range(2)]
        dwb = [at(f"dwb{i}", [128, 512], BF16, BB + 61440 + i * 1024) for i in range(2)]
        d_hi = [at(f"d_hi{i}", [128, TOK], BF16, BB + 0 + i * 2048) for i in range(2)]
        d_tf = at("d_tf", [128, TOK], BF16, BB + 4096)
        d_g0 = [at(f"d_g0{i}", [128, TOK], BF16, BB + 6144 + i * 2048) for i in range(2)]
        sc_ya = at("sc_ya", [128, 4, TB], F32, BB + 63488)
        sc_yt = at("sc_yt", [128, TB], F32, BB + 10240)
        sc_hCg = at("sc_hCg", [128, 4, TB], BF16, BB + 71680)
        sc_ytg = at("sc_ytg", [128, TB], F32, BB + 75776)
        sc_yp = at("sc_yp", [128, TB], F32, BB + 12288)
        sc_tg = at("sc_tg", [128, TB], BF16, BB + 14336)
        sc_gr = at("sc_gr", [128, TB], BF16, BB + 15360)
        opwb = [at(f"opwb{i}", [128, 32, 128], BF16, BB + 28672 + i * 8192) for i in range(2)]
        x1b = at("x1b", [128, 16, TOK], BF16, BB + 45056)
        g0ld = [at(f"g0ld{i}", [128, TOK], BF16, BB + 0 + i * 2048) for i in range(2)]
        xrld = [at(f"xrld{i}", [128, TOK], F32, BB + 4096 + i * 4096) for i in range(2)]
        x1st = at("x1st", [128, TOK], F32, BB + 77824)
        qf32 = [at(f"qf32{i}", [128, TOK], F32, BB + 28672 + i * 4096) for i in range(2)]
        elu_e = at("elu_e", [128, TOK], F32, BB + 36864)
        elu_m = at("elu_m", [128, TOK], F32, BB + 40960)
        wqkb = [at(f"wqkb{i}", [128, 16, 128], BF16, BB + 0 + i * 4096) for i in range(2)]
        tstg = [at(f"tstg{i}", [128, 128], BF16, BB + 8192 + i * 256) for i in range(2)]
        vstg = at("vstg", [128, 512], BF16, BB + 8704)
        kv_sb = at("kv_sb", [128, HID], F32, BB + 16384)
        qfm = at("qfm", [128, 16, TOK], BF16, BB + 49152)
        w13b = [at(f"w13b{i}", [128, 16, 128], BF16, BB + 0 + i * 4096) for i in range(4)]
        w2b = [at(f"w2b{i}", [128, 4, 128], BF16, BB + 16384 + i * 1024) for i in range(2)]
        m_t1 = at("m_t1", [128, TOK], BF16, BB + 18432)
        m_g1s = at("m_g1s", [128, TOK], BF16, BB + 20480)
        wrep = at("wrep", [128, NE, TOK], BF16, BB + 22528)
        kvbf = at("kvbf", [128, 16, 128], BF16, BB + 28672)
        nrst = at("nrst", [128, TOK], F32, BB + 0)
        g2st = at("g2st", [128, TOK], F32, BB + 4096)
        wob = [at(f"wob{i}", [128, 16, 128], BF16, BB + 8192 + i * 4096) for i in range(2)]
        u1ld = [at(f"u1ld{i}", [128, TOK], BF16, BB + 24576 + i * 2048) for i in range(2)]
        s1_sb = at("s1_sb", [128, 16, TOK], BF16, BB + 49152)

        def sb(name, shape, dt):
            return nc.alloc_sbuf_tensor(name, shape, dt)

        onesf = sb("onesf", [128, 128], F32)
        ones1b = sb("ones1b", [1, 128], BF16)
        onesb = sb("onesb", [128, 128], BF16)
        idf_sb = sb("idf_sb", [128, 128], F32)
        a_sb = sb("a_sb", [128, 4, NST], F32)
        dprm_sb = sb("dprm_sb", [128, 4], F32)
        dtb_sb = sb("dtb_sb", [128, 4], F32)
        gsc_sb = sb("gsc_sb", [128, 64], F32)
        gsch_sb = sb("gsch_sb", [128, 64], F32)
        guh_sb = sb("guh_sb", [128, 64], F32)
        usch_sb = sb("usch_sb", [128, 64], F32)
        dsc_sb = sb("dsc_sb", [128, 16], F32)
        dsch_sb = sb("dsch_sb", [128, 16], F32)
        bqk_sb = sb("bqk_sb", [128, 16], F32)
        bu_sb = sb("bu_sb", [128, 16], F32)
        buh_sb = sb("buh_sb", [128, 16], F32)
        bo_sb = sb("bo_sb", [128, 16], F32)
        lng_sb = sb("lng_sb", [128, 16], F32)
        lnb_sb = sb("lnb_sb", [128, 16], F32)
        bvr_sb = sb("bvr_sb", [1, HID], BF16)
        dtw_sb = sb("dtw_sb", [128, DL], BF16)
        xpw_sb = sb("xpw_sb", [128, 4, 160], BF16)
        hcar = sb("hcar", [128, 4, NST], F32)
        xtail = sb("xtail", [128, 4, 4], BF16)
        xn_buf = sb("xn_buf", [128, 16, TOK], BF16)
        u1st = sb("u1st", [128, TOK], BF16)
        vtmp = sb("vtmp", [128, TOK], F32)
        xsqb = sb("xsqb", [128, TOK], BF16)
        rwb = sb("rwb", [128, 16, NE], BF16)
        cwsm = sb("cwsm", [128, 4, 4], BF16)
        cbsm = sb("cbsm", [128, 4], BF16)
        zst4 = [at(f"zst4_{k}", [128, 4, 512], BF16, BB + 16448 + k * 4096)
                for k in range(2)]
        sco = sb("sco", [128, NE], F32)
        sst = sb("sst", [128, NE], F32)
        swt = sb("swt", [128, NE], F32)
        sd1 = sb("sd1", [128, 1], F32)
        sd2 = sb("sd2", [128, 1], F32)
        wtf = sb("wtf", [8, TOK], F32)
        eps_sb = sb("eps_sb", [128, 1], F32)
        m128_sb = sb("m128_sb", [128, 1], F32)

        pBig = ctx.enter_context(nc.psum_tensor("pBig", [128, 4096], F32))

        def pb(bank, n=512):
            return pBig[:, bank * 512: bank * 512 + n]

        def pb2(bank):
            return pBig[:, bank * 512: bank * 512 + 1024]

        sems = [Sem(s) for s in ["eT", "eV", "eV2", "eS", "eS2", "qC", "eG"]]
        eT, eV, eV2, eS, eS2, qC, eG = sems
        DStream._all = []
        st_x = DStream("st_x", 2)
        st_ng = DStream("st_ng", 2)
        st_gs = DStream("st_gs", 2)
        st_us = DStream("st_us", 2)
        st_ss = DStream("st_ss", 2)
        st_ds = DStream("st_ds", 2)
        st_sb = DStream("st_sb", 2)
        st_bc = DStream("st_bc", 2)
        st_ga = DStream("st_ga", 1)
        st_yf = DStream("st_yf", 2)
        st_h = DStream("st_h", 2)
        st_hl = DStream("st_hl", 2)
        st_g0 = DStream("st_g0", 2)
        st_ysb = DStream("st_ysb", 1)
        st_gl = DStream("st_gl", 2)
        st_xr = DStream("st_xr", 2)
        st_x1 = DStream("st_x1", 1)
        st_xl = DStream("st_xl", 2)
        st_u1 = DStream("st_u1", 1)
        st_ul = DStream("st_ul", 2)
        st_kv = DStream("st_kv", 1)
        st_wr = DStream("st_wr", 1)
        st_o = DStream("st_o", 2)
        st_xn = DStream("st_xn", 1)
        sg_sm = DStream("sg_sm", 1)
        sg_dt = DStream("sg_dt", 1)
        sg_w = DStream("sg_w", 2)
        sg_d = DStream("sg_d", 2)
        sg_op = DStream("sg_op", 2)
        sg_q = DStream("sg_q", 2)
        sg_13 = DStream("sg_13", 4)
        sg_w2 = DStream("sg_w2", 2)
        sg_kv = DStream("sg_kv", 1)
        sg_wt = DStream("sg_wt", 1)
        sg_wo = DStream("sg_wo", 2)
        for s in sems:
            s.h = ctx.enter_context(nc.semaphore(s.name))
        for ds_ in DStream._all:
            for s in ds_.slots:
                s.h = ctx.enter_context(nc.semaphore(s.name))

        T, V, S, G, Y = Rec("T"), Rec("V"), Rec("S"), Rec("G"), Rec("Y")
        S2, V2, Y2 = Rec("S2"), Rec("V2"), Rec("Y2")
        G2 = Rec("G2")

        bank_last = [None] * 8

        def bank_wait(rec, banks):
            for b in banks:
                if bank_last[b] is not None:
                    rec.wait(*bank_last[b])

        def bank_used(banks, sem, tgt):
            for b in banks:
                bank_last[b] = (sem, tgt)

        def bcast(dram_row):
            return AP(dram_row.tensor, dram_row.offset,
                      [[0, 128]] + list(dram_row.ap)[1:])

        for src, dst in [(alog, a_sb),
                         (dparam, dprm_sb), (dtbi, dtb_sb), (gsc, gsc_sb),
                         (dsc, dsc_sb), (bqk, bqk_sb), (bui, bu_sb),
                         (boi, bo_sb), (lng, lng_sb), (lnb, lnb_sb),
                         (idf, idf_sb), (bvr, bvr_sb), (dtw, dtw_sb),
                         (xpw, xpw_sb), (usc, usch_sb)]:
            sg_sm.issue(G, 0, dst[:], src[:])
        sg_sm.issue(G, 0, rwb[:], rw[:].rearrange("(kc p) e -> p kc e", p=128))
        sg_sm.issue(G, 0, ipw_sb[:], ipw[:].rearrange("(kc p) j -> p kc j", p=128))
        sg_sm.issue(G, 0, cwsm[:], convw[:])
        sg_sm.issue(G, 0, cbsm[:], convb[:])
        smalls = sg_sm.tgt()

        def cwb(k):
            s = cwsm[:, :, k]
            return AP(s.tensor, s.offset, list(s.ap) + [[0, 512]])
        _cb = cbsm[:, :]
        cbb = AP(_cb.tensor, _cb.offset, list(_cb.ap) + [[0, 512]])

        V.wait(*smalls)
        for t_, v_ in [(onesf, 1.0), (onesb, 1.0), (ones1b, 1.0),
                       (hcar, 0.0), (xtail, 0.0), (eps_sb, EPS),
                       (m128_sb, -128.0)]:
            V.memset(t_[:], v_).then_inc(eV, 1)
        V.tensor_scalar_mul(cwsm[:], cwsm[:], 0.5).then_inc(eV, 1)
        V.tensor_scalar_mul(cbsm[:], cbsm[:], 0.5).then_inc(eV, 1)
        V.tensor_scalar_mul(gsch_sb[:], gsc_sb[:], 0.5).then_inc(eV, 1)
        V.tensor_scalar_mul(usch_sb[:], usch_sb[:], 0.5).then_inc(eV, 1)
        V.tensor_scalar_mul(dsch_sb[:], dsc_sb[:], 0.5).then_inc(eV, 1)
        V.tensor_scalar_mul(buh_sb[:], bu_sb[:], 0.5).then_inc(eV, 1)
        V.tensor_mul(guh_sb[:], gsc_sb[:], usch_sb[:]).then_inc(eV, 1)
        S.wait(*smalls)
        S.activation(out=a_sb[:], in_=a_sb[:], func=AF.Exp).then_inc(eS, 1)
        V.wait(eS)
        V.tensor_scalar_mul(a_sb[:], a_sb[:], -1.0).then_inc(eV, 1)
        prep_eV = eV.n

        xch_free = [(None, 0), (None, 0)]
        for hc in range(16):
            sl = xch[hc % 2]
            ld = st_x.issue(Y, hc % 2, sl[:], x_t[hc * 128:(hc + 1) * 128, :],
                            gate=xch_free[hc % 2])
            S.wait(*ld)
            S.activation(out=xsqf[hc % 2][:], in_=sl[:], func=AF.Square
                         ).then_inc(eS, 1)
            sq = eS.n
            T.wait(*ld)
            T.wait(eS, sq)
            if hc == 0:
                T.wait(eV, prep_eV)
                bank_wait(T, [0, 1, 2, 3])
            for th in range(2):
                T.matmul(out=pb(0 + th), lhsT=onesf[:],
                         rhs=sl[:, th * 512:(th + 1) * 512],
                         start=(hc == 0), stop=(hc == 15))
                mm = T.matmul(out=pb(2 + th), lhsT=onesf[:],
                              rhs=xsqf[hc % 2][:, th * 512:(th + 1) * 512],
                              start=(hc == 0), stop=(hc == 15))
            mm.then_inc(eT, 1)
            xch_free[hc % 2] = (eT, eT.n)
        V.wait(eT)
        V.tensor_scalar_mul(mean_rep[:], pb2(0), 1.0 / HID).then_inc(eV, 1)
        V.tensor_scalar_mul(e2_rep[:], pb2(2), 1.0 / HID).then_inc(eV, 1)
        V.tensor_mul(vtmp[:], mean_rep[:], mean_rep[:]).then_inc(eV, 1)
        V.tensor_sub(vtmp[:], e2_rep[:], vtmp[:]).then_inc(eV, 1)
        bank_used([0, 1, 2, 3], eV, eV.n)
        S.wait(eV)
        S.activation(out=vtmp[:], in_=vtmp[:], func=AF.Ln, bias=eps_sb[:, 0:1]).then_inc(eS, 1)
        S.activation(out=rstd_rep[:], in_=vtmp[:], func=AF.Exp, scale=-0.5
                     ).then_inc(eS, 1)
        rstd0 = eS.n
        for hc in range(16):
            sl = xch[hc % 2]
            gate = xch_free[hc % 2] if hc < 2 else (eV, eV.n)
            ld = st_x.issue(Y, hc % 2, sl[:], x_t[hc * 128:(hc + 1) * 128, :],
                            gate=gate)
            V.wait(*ld)
            V.wait(eS, rstd0)
            V.tensor_sub(vtmp[:], sl[:], mean_rep[:]).then_inc(eV, 1)
            V.tensor_mul(xn_buf[:, hc, :], vtmp[:], rstd_rep[:]).then_inc(eV, 1)
        xn0_done = eV.n
        xn_st = st_xn.issue(Y, 0, g_xn_in[:], xn_buf[:], gate=(eV, xn0_done))
        G.wait(*xn_st)
        G.collective_compute("AllGather", ALU.bypass, replica_groups=RG,
                             ins=[g_xn_in], outs=[g_xn_out]).then_inc(qC, 1)
        ag_xn = qC.n

        T.wait(qC, ag_xn)
        Y.wait(qC, ag_xn)
        xng_free = [(None, 0), (None, 0)]
        gs_prev = [(None, 0), (None, 0)]
        ss_prev = [(None, 0), (None, 0)]
        usp_prev = [(None, 0), (None, 0)]
        uv_tgt = [0, 0]
        xproj_eT = [0, 0]
        for tt in range(16):
            buf = xng[tt % 2]
            r, lo = tt // 2, (tt % 2) * 512
            ld = st_ng.issue(Y, tt % 2, buf[:], g_xn_out[r, :, :, lo:lo + 512],
                             gate=xng_free[tt % 2])
            T.wait(*ld)
            for oc in range(8):
                bk = oc % 4
                bank_wait(T, [bk])
                for kc in range(16):
                    mm = T.matmul(out=pb(bk),
                                  lhsT=ipw_sb[:, kc, oc * 128:(oc + 1) * 128],
                                  rhs=buf[:, kc, :], start=(kc == 0), stop=(kc == 15))
                mm.then_inc(eT, 1)
                mt = eT.n
                if oc < 4:
                    cc = oc
                    S.wait(eT, mt)
                    S.copy(out=xcbuf[:, cc, 4:516], in_=pb(bk)).then_inc(eS, 1)
                    bank_used([bk], eS, eS.n)
                    if cc == 3:
                        V.wait(eS)
                        if tt > 1:
                            V.wait(usp_prev[tt % 2][0], usp_prev[tt % 2][1])
                            V.wait(eT, xproj_eT[tt % 2])
                        V.tensor_copy(out=xcbuf[:, :, 1:4], in_=xtail[:, :, 1:4]
                                      ).then_inc(eV, 1)
                        V.tensor_mul(zst4[0][:], xcbuf[:, :, 1:513], cwb(0)
                                     ).then_inc(eV, 1)
                        V.tensor_mul(zst4[1][:], xcbuf[:, :, 2:514], cwb(1)
                                     ).then_inc(eV, 1)
                        V.tensor_add(zst4[0][:], zst4[0][:], zst4[1][:]).then_inc(eV, 1)
                        V.tensor_mul(zst4[1][:], xcbuf[:, :, 3:515], cwb(2)
                                     ).then_inc(eV, 1)
                        V.tensor_add(zst4[0][:], zst4[0][:], zst4[1][:]).then_inc(eV, 1)
                        V.tensor_mul(zst4[1][:], xcbuf[:, :, 4:516], cwb(3)
                                     ).then_inc(eV, 1)
                        V.tensor_add(zst4[1][:], zst4[1][:], cbb).then_inc(eV, 1)
                        V.tensor_add(zst4[0][:], zst4[0][:], zst4[1][:]).then_inc(eV, 1)
                        V.tensor_copy(out=xtail[:, :, 1:4], in_=xcbuf[:, :, 513:516]
                                      ).then_inc(eV, 1)
                        S.wait(eV)
                        S.activation(out=zst4[1][:], in_=zst4[0][:], func=AF.Tanh
                                     ).then_inc(eS, 1)
                        V.wait(eS)
                        V.scalar_tensor_tensor(out=ubuf[tt % 2][:], in0=zst4[1][:],
                                               scalar=1.0, in1=zst4[0][:],
                                               op0=ALU.add, op1=ALU.mult
                                               ).then_inc(eV, 1)
                else:
                    gg = gstg[oc % 2]
                    if gs_prev[oc % 2][0] is not None:
                        S.wait(*gs_prev[oc % 2])
                    S.wait(eT, mt)
                    S.copy(out=gg[:], in_=pb(bk)).then_inc(eS, 1)
                    bank_used([bk], eS, eS.n)
                    gs_prev[oc % 2] = st_gs.issue(
                        Y, oc % 2, sp_gate[:, oc - 4, tt * 512:(tt + 1) * 512],
                        gg[:], gate=(eS, eS.n))
            xng_free[tt % 2] = (eT, eT.n)
            uv_tgt[tt % 2] = eV.n
            usp_prev[tt % 2] = st_us.issue(
                Y, tt % 2, sp_u[:, :, tt * 512:(tt + 1) * 512],
                ubuf[tt % 2][:], gate=(eV, eV.n))
            for lt in ([tt - 1] if tt > 0 else []) + ([tt] if tt == 15 else []):
                sl_ = lt % 2
                bank_wait(T, [4, 5])
                T.wait(eV, uv_tgt[sl_])
                for cc in range(4):
                    T.matmul(out=pb(4), lhsT=xpw_sb[:, cc, 0:128],
                             rhs=ubuf[sl_][:, cc, :],
                             start=(cc == 0), stop=(cc == 3))
                    mm = T.matmul(out=pBig[0:32, 5 * 512:5 * 512 + 512],
                                  lhsT=xpw_sb[:, cc, 128:160],
                                  rhs=ubuf[sl_][:, cc, :],
                                  start=(cc == 0), stop=(cc == 3))
                mm.then_inc(eT, 1)
                xproj_eT[sl_] = eT.n
                if ss_prev[sl_][0] is not None:
                    S.wait(*ss_prev[sl_])
                S.wait(eT)
                S.copy(out=dstg[sl_][:], in_=pb(4)).then_inc(eS, 1)
                S.copy(out=bcst[sl_][:], in_=pBig[0:32, 5 * 512:5 * 512 + 512]
                       ).then_inc(eS, 1)
                bank_used([4, 5], eS, eS.n)
                st_ss.issue(Y, sl_, g_ssm_in[0:128, lt * 512:(lt + 1) * 512],
                            dstg[sl_][:], gate=(eS, eS.n))
                ss_prev[sl_] = st_ss.issue(
                    Y, sl_, g_ssm_in[128:160, lt * 512:(lt + 1) * 512],
                    bcst[sl_][:])
        G.wait(*st_ss.tgt(0))
        G.wait(*st_ss.tgt(1))
        G.wait(*st_us.tgt(0))
        G.wait(*st_gs.tgt(0))
        G.wait(*st_gs.tgt(1))
        G.collective_compute("AllReduce", ALU.add, replica_groups=RG,
                             ins=[g_ssm_in], outs=[g_ssm_out]).then_inc(qC, 1)
        ar_ssm = qC.n
        G.wait(qC, ar_ssm)
        sg_dt.issue(G, 0, dtlr_sb[:], g_ssm_out[0:128, :])
        dtlr_ld = sg_dt.issue(G, 0, g_bc[:], g_ssm_out[128:160, :])

        T.wait(*dtlr_ld)
        S.wait(*st_ss.tgt(0))
        S.wait(*st_ss.tgt(1))
        ds_prev = [(None, 0), (None, 0)]
        ds_rd = [(None, 0), (None, 0)]
        for cc in range(4):
            for tq in range(16):
                bk = (cc * 16 + tq) % 4
                bank_wait(T, [bk])
                mm = T.matmul(out=pb(bk), lhsT=dtw_sb[:, cc * 128:(cc + 1) * 128],
                              rhs=dtlr_sb[:, tq * 512:(tq + 1) * 512],
                              start=True, stop=True)
                mm.then_inc(eT, 1)
                mt = eT.n
                st1 = dstg[tq % 2]
                st2 = dstg2[tq % 2]
                S.wait(eT, mt)
                if ds_rd[tq % 2][0] is not None:
                    S.wait(*ds_rd[tq % 2])
                S.activation(out=st1[:], in_=pb(bk), func=AF.Abs,
                             bias=dtb_sb[:, cc:cc + 1]).then_inc(eS, 1)
                S.activation(out=st1[:], in_=st1[:], func=AF.Exp, scale=-1.0
                             ).then_inc(eS, 1)
                S.activation(out=st1[:], in_=st1[:], func=AF.Ln, bias=1.0
                             ).then_inc(eS, 1)
                V.wait(eS)
                if ds_prev[tq % 2][0] is not None:
                    V.wait(*ds_prev[tq % 2])
                V.tensor_scalar(out=vtmp[:, 0:512], in0=pb(bk),
                                scalar1=dtb_sb[:, cc:cc + 1], scalar2=0.0,
                                op0=ALU.add, op1=ALU.max).then_inc(eV, 1)
                V.tensor_add(st2[:], vtmp[:, 0:512], st1[:]).then_inc(eV, 1)
                ds_rd[tq % 2] = (eV, eV.n)
                bank_used([bk], eV, eV.n)
                ds_prev[tq % 2] = st_ds.issue(
                    Y, tq % 2, sp_delta[:, cc, tq * 512:(tq + 1) * 512],
                    st2[:], gate=(eV, eV.n))

        gwb_free = [(None, 0), (None, 0)]
        hsp_prev = [(None, 0), (None, 0)]
        stg_rd = [(None, 0), (None, 0)]
        unit_marks = []
        G2.wait(eV, eV.n)
        G2.wait(eS, eS.n)
        G2.wait(*st_ds.tgt(0))
        G2.wait(*st_ds.tgt(1))
        for p in range(32):
            sl = p % 2
            sg_w.issue(G2, sl, gwb[sl][:], gw[:, p * 256:(p + 1) * 256]
                       .rearrange("(kc q) j -> q kc j", q=128), gate=gwb_free[sl])
            wl = sg_w.issue(G2, sl, uwb[sl][:], uw[:, p * 256:(p + 1) * 256]
                            .rearrange("(kc q) j -> q kc j", q=128))
            S2.wait(*wl)
            S2.op("add", gwb[sl][:], gwb[sl][:], m128_sb[:, 0:1]).then_inc(eS2, 1)
            S2.op("add", uwb[sl][:], uwb[sl][:], m128_sb[:, 0:1]).then_inc(eS2, 1)
            sh = eS2.n
            T.wait(eS2, sh)
            for s_ in range(2):
                i = 2 * p + s_
                isl = i % 2
                banks = [0, 1, 2, 3] if s_ == 0 else [4, 5, 6, 7]
                bank_wait(T, banks)
                for th in range(2):
                    for kc in range(16):
                        mm = T.matmul(out=pb(banks[0] + th),
                                      lhsT=gwb[sl][:, kc, s_ * 128:(s_ + 1) * 128],
                                      rhs=xn_buf[:, kc, th * 512:(th + 1) * 512],
                                      start=(kc == 0), stop=(kc == 15))
                for th in range(2):
                    for kc in range(16):
                        mm = T.matmul(out=pb(banks[2] + th),
                                      lhsT=uwb[sl][:, kc, s_ * 128:(s_ + 1) * 128],
                                      rhs=xn_buf[:, kc, th * 512:(th + 1) * 512],
                                      start=(kc == 0), stop=(kc == 15))
                mm.then_inc(eT, 1)
                mt = eT.n
                S2.wait(eT, mt)
                if stg_rd[isl][0] is not None:
                    S2.wait(*stg_rd[isl])
                S2.activation(out=f_tg[isl][:], in_=pb2(banks[0]), func=AF.Tanh,
                              scale=gsch_sb[:, i:i + 1]).then_inc(eS2, 1)
                ev = eS2.n
                V2.wait(eS2, ev)
                if hsp_prev[isl][0] is not None:
                    V2.wait(*hsp_prev[isl])
                V2.scalar_tensor_tensor(out=f_gv[isl][:], in0=f_tg[isl][:],
                                        scalar=1.0, in1=pb2(banks[0]),
                                        op0=ALU.add, op1=ALU.mult).then_inc(eV2, 1)
                V2.tensor_mul(f_h[isl][:], f_gv[isl][:], pb2(banks[2])
                              ).then_inc(eV2, 1)
                V2.tensor_scalar(out=f_h[isl][:], in0=f_h[isl][:],
                                 scalar1=guh_sb[:, i:i + 1], scalar2=None,
                                 op0=ALU.mult).then_inc(eV2, 1)
                stg_rd[isl] = (eV2, eV2.n)
                bank_used(banks, eV2, eV2.n)
                hsp_prev[isl] = st_h.issue(Y2, isl, sp_h[i, :, :], f_h[isl][:],
                                           gate=(eV2, eV2.n))
                unit_marks.append((len(S2.calls), len(V2.calls), len(Y2.calls), len(G2.calls)))
            gwb_free[sl] = (eT, eT.n)
        hspill_all = [st_h.tgt(0), st_h.tgt(1)]

        dwb_free = [(None, 0), (None, 0)]
        dhi_free = [(None, 0), (None, 0)]
        g0_prev = [(None, 0), (None, 0)]
        dstg_rd = (None, 0)
        for ob in range(4):
            for i in range(64):
                sl = i % 2
                wl = sg_d.issue(G2, sl, dwb[sl][:],
                                dwn[i * 128:(i + 1) * 128, ob * 512:(ob + 1) * 512],
                                gate=dwb_free[sl])
                S2.wait(*wl)
                S2.op("add", dwb[sl][:], dwb[sl][:], m128_sb[:, 0:1]).then_inc(eS2, 1)
                sh = eS2.n
                if ob == 0 and i < 2:
                    Y2.wait(*hspill_all[0])
                    Y2.wait(*hspill_all[1])
                hl = st_hl.issue(Y2, sl, d_hi[sl][:], sp_h[i, :, :],
                                 gate=dhi_free[sl])
                T.wait(eS2, sh)
                T.wait(*hl)
                if i == 0:
                    bank_wait(T, [0, 1, 2, 3, 4, 5, 6, 7])
                for ol in range(4):
                    for th in range(2):
                        mm = T.matmul(out=pb(2 * ol + th),
                                      lhsT=dwb[sl][:, ol * 128:(ol + 1) * 128],
                                      rhs=d_hi[sl][:, th * 512:(th + 1) * 512],
                                      start=(i == 0), stop=(i == 63))
                mm.then_inc(eT, 1)
                dwb_free[sl] = (eT, eT.n)
                dhi_free[sl] = (eT, eT.n)
                unit_marks.append((len(S2.calls), len(V2.calls), len(Y2.calls), len(G2.calls)))
            mt = eT.n
            for ol in range(4):
                o = ob * 4 + ol
                S2.wait(eT, mt)
                if dstg_rd[0] is not None:
                    S2.wait(*dstg_rd)
                S2.activation(out=d_tf[:], in_=pb2(2 * ol), func=AF.Tanh,
                              scale=dsch_sb[:, o:o + 1]).then_inc(eS2, 1)
                ev = eS2.n
                V2.wait(eS2, ev)
                if g0_prev[ol % 2][0] is not None:
                    V2.wait(*g0_prev[ol % 2])
                V2.scalar_tensor_tensor(out=d_g0[ol % 2][:], in0=d_tf[:], scalar=1.0,
                                        in1=pb2(2 * ol), op0=ALU.add, op1=ALU.mult
                                        ).then_inc(eV2, 1)
                V2.tensor_scalar(out=d_g0[ol % 2][:], in0=d_g0[ol % 2][:],
                                 scalar1=dsc_sb[:, o:o + 1], scalar2=None,
                                 op0=ALU.mult).then_inc(eV2, 1)
                dstg_rd = (eV2, eV2.n)
                g0_prev[ol % 2] = st_g0.issue(Y2, ol % 2, sp_g0[o, :, :],
                                              d_g0[ol % 2][:], gate=(eV2, eV2.n))
                if ol == 3:
                    bank_used([0, 1, 2, 3, 4, 5, 6, 7], eV2, eV2.n)
            unit_marks.append((len(S2.calls), len(V2.calls), len(Y2.calls), len(G2.calls)))

        s2_state = [0, len(S2.calls)]
        v2_state = [0, len(V2.calls)]
        y2_state = [0, len(Y2.calls)]
        g2_state = [0, len(G2.calls)]
        de_free = [(None, 0), (None, 0)]
        ga_free = (None, 0)
        rep_free = [(None, 0), (None, 0)]
        yf_prev = [(None, 0), (None, 0)]
        tg_rd = (None, 0)
        dA_free = [(None, 0)] * 8
        sch_rd = [(None, 0)] * 4
        epi_rd = [None] * 4
        g_done = [None] * 4
        rot8 = 0
        for b in range(NB):
            sl = b % 2
            if b == 0:
                Y.wait(*st_ds.tgt(0))
                Y.wait(*st_ds.tgt(1))
                Y.wait(*st_us.tgt(0))
                Y.wait(*st_us.tgt(1))
            st_sb.issue(Y, sl, sc_de[sl][:], sp_delta[:, :, b * TB:(b + 1) * TB],
                        gate=de_free[sl])
            blk_ld = st_sb.issue(Y, sl, sc_u[sl][:], sp_u[:, :, b * TB:(b + 1) * TB])
            def issue_bcast(nq, gate):
                qsl = nq % 2
                for n4 in range(4):
                    n = nq * 4 + n4
                    st_bc.issue(Y, qsl, brep[qsl][:, n4, :],
                                bcast(g_bc[n:n + 1, b * TB:(b + 1) * TB]),
                                gate=gate if n4 == 0 else None)
                    st_bc.issue(Y, qsl, crep[qsl][:, n4, :],
                                bcast(g_bc[16 + n:17 + n, b * TB:(b + 1) * TB]))
                return st_bc.tgt(qsl)

            nq_tgt = [None] * 4
            for nq0 in range(2):
                gate0 = rep_free[nq0 % 2] if b > 0 else (
                    dtlr_ld if nq0 == 0 else (None, 0))
                nq_tgt[nq0] = issue_bcast(nq0, gate0)
            if b == 0:
                Y.wait(*st_gs.tgt(0))
                Y.wait(*st_gs.tgt(1))
            ga_ld = st_ga.issue(Y, 0, sc_ga[:], sp_gate[:, :, b * TB:(b + 1) * TB],
                                gate=ga_free)

            V.wait(*blk_ld)
            V.tensor_mul(sc_du[:], sc_de[sl][:], sc_u[sl][:]).then_inc(eV, 1)
            for nq in range(4):
                qsl = nq % 2
                V.wait(*nq_tgt[nq])
                for cc in range(4):
                    for n4 in range(4):
                        n = nq * 4 + n4
                        r = rot8 % 8
                        rb = rot8 % 2
                        rh = rot8 % 4
                        w = dA_free[r]
                        if w[0] is not None:
                            S.wait(w[0], w[1])
                        S.wait(*blk_ld)
                        S.activation(out=sc_dA[r][:], in_=sc_de[sl][:, cc, :],
                                     func=AF.Exp, scale=a_sb[:, cc, n:n + 1]
                                     ).then_inc(eS, 1)
                        da = eS.n
                        V.tensor_mul(sc_b[rb][:], sc_du[:, cc, :], brep[qsl][:, n4, :]
                                     ).then_inc(eV, 1)
                        V.wait(eS, da)
                        if sch_rd[rh][0] is not None:
                            V.wait(*sch_rd[rh])
                        V.tensor_tensor_scan(out=sc_h[rh][:], data0=sc_dA[r][:],
                                             data1=sc_b[rb][:],
                                             initial=hcar[:, cc, n:n + 1],
                                             op0=ALU.mult, op1=ALU.add
                                             ).then_inc(eV, 1)
                        tts_v = eV.n
                        V.tensor_copy(out=hcar[:, cc, n:n + 1],
                                      in_=sc_h[rh][:, TB - 1:TB]).then_inc(eV, 1)
                        if cc == 0:
                            V.tensor_mul(sc_hC[:, n4, :], sc_h[rh][:],
                                         crep[qsl][:, n4, :]).then_inc(eV, 1)
                            sch_rd[rh] = (None, 0)
                        else:
                            G.wait(eV, tts_v)
                            G.tensor_mul(sc_hCg[:, n4, :], sc_h[rh][:],
                                         crep[qsl][:, n4, :]).then_inc(eG, 1)
                            sch_rd[rh] = (eG, eG.n)
                        dA_free[r] = (eV, eV.n)
                        rot8 += 1
                    if cc == 0:
                        if nq == 0:
                            V.tensor_reduce(out=sc_ya[:, cc, :],
                                            in_=sc_hC[:].rearrange("p n t -> p t n"),
                                            axis=mybir.AxisListType.X, op=ALU.add
                                            ).then_inc(eV, 1)
                        else:
                            V.tensor_reduce(out=sc_yt[:],
                                            in_=sc_hC[:].rearrange("p n t -> p t n"),
                                            axis=mybir.AxisListType.X, op=ALU.add
                                            ).then_inc(eV, 1)
                            V.tensor_add(sc_ya[:, cc, :], sc_ya[:, cc, :], sc_yt[:]
                                         ).then_inc(eV, 1)
                    else:
                        if nq == 0:
                            if epi_rd[cc] is not None:
                                G.wait(eV, epi_rd[cc])
                            G.tensor_add(sc_ya[:, cc, :], sc_hCg[:, 0, :],
                                         sc_hCg[:, 1, :]).then_inc(eG, 1)
                            G.tensor_add(sc_ytg[:], sc_hCg[:, 2, :],
                                         sc_hCg[:, 3, :]).then_inc(eG, 1)
                            G.tensor_add(sc_ya[:, cc, :], sc_ya[:, cc, :],
                                         sc_ytg[:]).then_inc(eG, 1)
                        else:
                            G.tensor_add(sc_ytg[:], sc_hCg[:, 0, :],
                                         sc_hCg[:, 1, :]).then_inc(eG, 1)
                            G.tensor_add(sc_ya[:, cc, :], sc_ya[:, cc, :],
                                         sc_ytg[:]).then_inc(eG, 1)
                            G.tensor_add(sc_ytg[:], sc_hCg[:, 2, :],
                                         sc_hCg[:, 3, :]).then_inc(eG, 1)
                            G.tensor_add(sc_ya[:, cc, :], sc_ya[:, cc, :],
                                         sc_ytg[:]).then_inc(eG, 1)
                            if nq == 3:
                                g_done[cc] = (eG, eG.n)
                rep_free[qsl] = (eG, eG.n)
                if nq < 2:
                    nq_tgt[nq + 2] = issue_bcast(nq + 2, rep_free[qsl])
                nu = len(unit_marks)
                u = min(nu, int(np.ceil(nu * (b + (nq + 1) / 4.0) / 13.0)))
                if u > 0:
                    mark = unit_marks[u - 1]
                    splice_to(S, S2, s2_state, mark[0])
                    splice_to(V, V2, v2_state, mark[1])
                    splice_to(Y, Y2, y2_state, mark[2])
                    splice_to(G, G2, g2_state, mark[3])
            for cc in range(4):
                V.tensor_scalar(out=sc_yp[:], in0=sc_u[sl][:, cc, :],
                                scalar1=dprm_sb[:, cc:cc + 1], scalar2=None,
                                op0=ALU.mult).then_inc(eV, 1)
                if cc > 0:
                    V.wait(*g_done[cc])
                V.tensor_add(sc_yp[:], sc_yp[:], sc_ya[:, cc, :]).then_inc(eV, 1)
                epi_rd[cc] = eV.n
                S.wait(*ga_ld)
                if tg_rd[0] is not None:
                    S.wait(*tg_rd)
                S.activation(out=sc_tg[:], in_=sc_ga[:, cc, :], func=AF.Tanh,
                             scale=0.5).then_inc(eS, 1)
                V.wait(eS)
                V.scalar_tensor_tensor(out=sc_gr[:], in0=sc_tg[:], scalar=1.0,
                                       in1=sc_ga[:, cc, :], op0=ALU.add,
                                       op1=ALU.mult).then_inc(eV, 1)
                tg_rd = (eV, eV.n)
                if yf_prev[cc % 2][0] is not None:
                    V.wait(*yf_prev[cc % 2])
                V.tensor_mul(sc_yf[cc % 2][:], sc_yp[:], sc_gr[:]).then_inc(eV, 1)
                yf_prev[cc % 2] = st_yf.issue(
                    Y, cc % 2,
                    g_y_in[b // 2, :, cc, (b % 2) * TB:(b % 2) * TB + TB],
                    sc_yf[cc % 2][:], gate=(eV, eV.n))
            de_free[sl] = (eV, eV.n)
            ga_free = (eV, eV.n)
        splice_to(S, S2, s2_state, len(S2.calls))
        splice_to(V, V2, v2_state, len(V2.calls))
        splice_to(Y, Y2, y2_state, len(Y2.calls))
        splice_to(G, G2, g2_state, len(G2.calls))
        G.wait(*st_yf.tgt(0))
        G.wait(*st_yf.tgt(1))
        with allow_ici_all_to_all():
            G.collective_compute("AllToAll", ALU.bypass, replica_groups=RG,
                                 ins=[g_y_in], outs=[g_y_out]).then_inc(qC, 1)
        a2a_y = qC.n

        Y.wait(qC, a2a_y)
        for r in range(NCORE):
            ysb_ld = st_ysb.issue(Y, 0, y_sb[:, r * 4:(r + 1) * 4, :],
                                  g_y_out[r, :, :, :])
        opw_free = [(None, 0), (None, 0)]
        g0_free = [(None, 0), (None, 0)]
        xr_free = [(None, 0), (None, 0)]
        T.wait(*ysb_ld)
        for o in range(16):
            sl = o % 2
            wl = sg_op.issue(G, sl, opwb[sl][:], opw[:, o * 128:(o + 1) * 128]
                             .rearrange("(dc p) j -> p dc j", p=128),
                             gate=opw_free[sl])
            if o == 0:
                Y.wait(*st_g0.tgt(0))
                Y.wait(*st_g0.tgt(1))
            st_gl.issue(Y, sl, g0ld[sl][:], sp_g0[o, :, :], gate=g0_free[sl])
            ld = st_xr.issue(Y, sl, xrld[sl][:], x_t[o * 128:(o + 1) * 128, :],
                             gate=xr_free[sl])
            gld = st_gl.tgt(sl)
            bks = [0, 1] if sl == 0 else [2, 3]
            T.wait(*wl)
            bank_wait(T, bks)
            for dc in range(32):
                for th in range(2):
                    mm = T.matmul(out=pb(bks[0] + th),
                                  lhsT=opwb[sl][:, dc, :],
                                  rhs=y_sb[:, dc, th * 512:(th + 1) * 512],
                                  start=(dc == 0), stop=(dc == 31))
            mm.then_inc(eT, 1)
            mt = eT.n
            opw_free[sl] = (eT, mt)
            V.wait(eT, mt)
            V.wait(*ld)
            V.wait(*gld)
            if o >= 1:
                V.wait(*st_x1.tgt(0))
            V.tensor_mul(vtmp[:], pb2(bks[0]), g0ld[sl][:]).then_inc(eV, 1)
            V.scalar_tensor_tensor(out=x1st[:], in0=vtmp[:], scalar=0.25,
                                   in1=xrld[sl][:], op0=ALU.mult, op1=ALU.add
                                   ).then_inc(eV, 1)
            x1v = eV.n
            bank_used(bks, eV, x1v)
            g0_free[sl] = (eV, x1v)
            xr_free[sl] = (eV, x1v)
            st_x1.issue(Y, 0, sp_x1[o, :, :], x1st[:], gate=(eV, x1v))
            S.wait(eV, x1v)
            S.copy(out=x1b[:, o, :], in_=x1st[:]).then_inc(eS, 1)
            S.activation(out=xsqb[:], in_=x1b[:, o, :], func=AF.Square
                         ).then_inc(eS, 1)
            sq = eS.n
            T.wait(eS, sq)
            if o == 0:
                bank_wait(T, [4, 5, 6, 7])
            for th in range(2):
                T.matmul(out=pb(4 + th), lhsT=onesb[:],
                         rhs=x1b[:, o, th * 512:(th + 1) * 512],
                         start=(o == 0), stop=(o == 15))
                mm = T.matmul(out=pb(6 + th), lhsT=onesb[:],
                              rhs=xsqb[:, th * 512:(th + 1) * 512],
                              start=(o == 0), stop=(o == 15))
            mm.then_inc(eT, 1)
        V.wait(eT)
        V.tensor_scalar_mul(mean_rep[:], pb2(4), 1.0 / HID).then_inc(eV, 1)
        V.tensor_scalar_mul(e2_rep[:], pb2(6), 1.0 / HID).then_inc(eV, 1)
        V.tensor_mul(vtmp[:], mean_rep[:], mean_rep[:]).then_inc(eV, 1)
        V.tensor_sub(vtmp[:], e2_rep[:], vtmp[:]).then_inc(eV, 1)
        bank_used([4, 5, 6, 7], eV, eV.n)
        S.wait(eV)
        S.activation(out=vtmp[:], in_=vtmp[:], func=AF.Ln, bias=eps_sb[:, 0:1]).then_inc(eS, 1)
        S.activation(out=rstd_rep[:], in_=vtmp[:], func=AF.Exp, scale=-0.5
                     ).then_inc(eS, 1)
        rstd1 = eS.n
        for hc in range(16):
            V.wait(eS, rstd1)
            V.tensor_sub(vtmp[:], x1b[:, hc, :], mean_rep[:]).then_inc(eV, 1)
            V.tensor_mul(xn_buf[:, hc, :], vtmp[:], rstd_rep[:]).then_inc(eV, 1)
        xn1_done = eV.n

        wqk_free = [(None, 0), (None, 0)]
        T.wait(eV, xn1_done)
        G.wait(eV, xn1_done)
        S.wait(*st_x1.tgt(0))
        for oc in range(16):
            sl = oc % 2
            wl = sg_q.issue(G, sl, wqkb[sl][:], wqk[:, oc * 128:(oc + 1) * 128]
                            .rearrange("(kc p) j -> p kc j", p=128),
                            gate=wqk_free[sl])
            bks = [0, 1] if sl == 0 else [2, 3]
            T.wait(*wl)
            bank_wait(T, bks)
            for kc in range(16):
                for th in range(2):
                    mm = T.matmul(out=pb(bks[0] + th),
                                  lhsT=wqkb[sl][:, kc, :],
                                  rhs=xn_buf[:, kc, th * 512:(th + 1) * 512],
                                  start=(kc == 0), stop=(kc == 15))
            mm.then_inc(eT, 1)
            mt = eT.n
            wqk_free[sl] = (eT, mt)
            V.wait(eT, mt)
            if oc >= 2:
                V.wait(eS)
            V.tensor_scalar(out=elu_m[:], in0=pb2(bks[0]),
                            scalar1=bqk_sb[:, oc:oc + 1], scalar2=0.0,
                            op0=ALU.add, op1=ALU.min).then_inc(eV, 1)
            V.tensor_scalar(out=elu_e[:], in0=pb2(bks[0]),
                            scalar1=bqk_sb[:, oc:oc + 1], scalar2=0.0,
                            op0=ALU.add, op1=ALU.max).then_inc(eV, 1)
            bank_used(bks, eV, eV.n)
            S.wait(eV)
            S.activation(out=elu_m[:], in_=elu_m[:], func=AF.Exp).then_inc(eS, 1)
            V.wait(eS)
            V.scalar_tensor_tensor(out=qf32[sl][:], in0=elu_m[:], scalar=-1.0,
                                   in1=elu_e[:], op0=ALU.add, op1=ALU.add
                                   ).then_inc(eV, 1)
            qv = eV.n
            S.wait(eV, qv)
            S.copy(out=qfm[:, oc, :], in_=qf32[sl][:]).then_inc(eS, 1)
            T.wait(eV, qv)
            for tc in range(8):
                tb_ = 4 + 2 * sl + (tc % 2)
                bank_wait(T, [tb_])
                T.transpose(out=pb(tb_, 128), in_=qf32[sl][:, tc * 128:(tc + 1) * 128],
                            identity=idf_sb[:]).then_inc(eT, 1)
                S.wait(eT)
                S.copy(out=qtok[:, tc, oc, :], in_=pb(tb_, 128)).then_inc(eS, 1)
                bank_used([tb_], eS, eS.n)
        wv_free = list(wqk_free)
        wvb = wqkb
        for hg in range(16):
            sl = hg % 2
            wl = sg_q.issue(G, sl, wvb[sl][:], wv[:, hg * 128:(hg + 1) * 128]
                            .rearrange("(kc p) j -> p kc j", p=128),
                            gate=wv_free[sl])
            T.wait(*wl)
            for tc in range(8):
                bk = (hg * 8 + tc) % 4
                bank_wait(T, [bk])
                for kc in range(16):
                    mm = T.matmul(out=pb(bk, 128),
                                  lhsT=xn_buf[:, kc, tc * 128:(tc + 1) * 128],
                                  rhs=wvb[sl][:, kc, :],
                                  start=(kc == 0), stop=False)
                mm = T.matmul(out=pb(bk, 128), lhsT=ones1b[:],
                              rhs=bvr_sb[:, hg * 128:(hg + 1) * 128],
                              start=False, stop=True)
                mm.then_inc(eT, 1)
                mt = eT.n
                S.wait(eT, mt)
                if hg + tc > 0:
                    S.wait(eV)
                S.activation(out=tstg[0][:], in_=pb(bk, 128), func=AF.Tanh,
                             scale=0.5).then_inc(eS, 1)
                V.wait(eS)
                V.scalar_tensor_tensor(out=vtok[:, tc, hg * 128:(hg + 1) * 128],
                                       in0=tstg[0][:], scalar=1.0,
                                       in1=pb(bk, 128), op0=ALU.add, op1=ALU.mult
                                       ).then_inc(eV, 1)
                bank_used([bk], eV, eV.n)
            wv_free[sl] = (eT, eT.n)
        v_done = eV.n
        T.wait(eV, v_done)
        for hb in range(4):
            bank_wait(T, [4, 5, 6, 7])
            for tc in range(8):
                for hh in range(4):
                    h = hb * 4 + hh
                    mm = T.matmul(out=pb(4 + hh, 128),
                                  lhsT=qtok[:, tc, h, :],
                                  rhs=vtok[:, tc, h * 128:(h + 1) * 128],
                                  start=(tc == 0), stop=(tc == 7))
            mm.then_inc(eT, 1)
            mt = eT.n
            S.wait(eT, mt)
            for hh in range(4):
                h = hb * 4 + hh
                S.copy(out=kv_sb[:, h * 128:(h + 1) * 128], in_=pb(4 + hh, 128)
                       ).then_inc(eS, 1)
            bank_used([4, 5, 6, 7], eS, eS.n)
        kv_st = st_kv.issue(Y, 0, g_kv_in[:], kv_sb[:], gate=(eS, eS.n))
        G.wait(*kv_st)
        G.collective_compute("AllReduce", ALU.add, replica_groups=RG,
                             ins=[g_kv_in], outs=[g_kv_out]).then_inc(qC, 1)
        ar_kv = qC.n

        wu_free = list(wv_free)
        for oc in range(16):
            sl = oc % 2
            wl = sg_q.issue(G, sl, wqkb[sl][:], wu[:, oc * 128:(oc + 1) * 128]
                            .rearrange("(kc p) j -> p kc j", p=128),
                            gate=wu_free[sl])
            bks = [0, 1] if sl == 0 else [2, 3]
            T.wait(*wl)
            bank_wait(T, bks)
            for kc in range(16):
                for th in range(2):
                    mm = T.matmul(out=pb(bks[0] + th),
                                  lhsT=wqkb[sl][:, kc, :],
                                  rhs=xn_buf[:, kc, th * 512:(th + 1) * 512],
                                  start=(kc == 0), stop=(kc == 15))
            mm.then_inc(eT, 1)
            mt = eT.n
            wu_free[sl] = (eT, mt)
            S.wait(eT, mt)
            if oc > 0:
                S.wait(eV)
            S.activation(out=m_t1[:], in_=pb2(bks[0]), func=AF.Tanh, scale=0.5,
                         bias=buh_sb[:, oc:oc + 1]).then_inc(eS, 1)
            ev = eS.n
            V.wait(eS, ev)
            V.tensor_scalar(out=vtmp[:], in0=pb2(bks[0]),
                            scalar1=bu_sb[:, oc:oc + 1], scalar2=None,
                            op0=ALU.add).then_inc(eV, 1)
            if oc >= 1:
                V.wait(*st_u1.tgt(0))
            V.scalar_tensor_tensor(out=u1st[:], in0=m_t1[:], scalar=1.0,
                                   in1=vtmp[:], op0=ALU.add, op1=ALU.mult
                                   ).then_inc(eV, 1)
            uv = eV.n
            bank_used(bks, eV, uv)
            st_u1.issue(Y, 0, sp_u1[oc, :, :], u1st[:], gate=(eV, uv))
        T.wait(*smalls)
        for tc in range(8):
            bk = tc % 4
            bank_wait(T, [bk])
            for kc in range(16):
                mm = T.matmul(out=pBig[:, bk * 512:bk * 512 + NE],
                              lhsT=xn_buf[:, kc, tc * 128:(tc + 1) * 128],
                              rhs=rwb[:, kc, :], start=(kc == 0), stop=(kc == 15))
            mm.then_inc(eT, 1)
            mt = eT.n
            V.wait(eT, mt)
            V.tensor_copy(out=sco[:], in_=pBig[:, bk * 512:bk * 512 + NE]
                          ).then_inc(eV, 1)
            V.op("max", out=sst[:], in_=sco[:]).then_inc(eV, 1)
            V.tensor_scalar(out=swt[:], in0=sco[:], scalar1=sst[:, 0:1],
                            scalar2=None, op0=ALU.subtract).then_inc(eV, 1)
            V.tensor_scalar(out=sd1[:], in0=sst[:, 1:2], scalar1=sst[:, 0:1],
                            scalar2=None, op0=ALU.subtract).then_inc(eV, 1)
            bank_used([bk], eV, eV.n)
            S.wait(eV)
            S.activation(out=swt[:], in_=swt[:], func=AF.Exp).then_inc(eS, 1)
            S.activation(out=sd1[:], in_=sd1[:], func=AF.Exp).then_inc(eS, 1)
            sx = eS.n
            V.wait(eS, sx)
            V.tensor_scalar(out=sd2[:], in0=sd1[:], scalar1=1.0, scalar2=2.0,
                            op0=ALU.add, op1=ALU.mult).then_inc(eV, 1)
            V.reciprocal(out=sd2[:], in_=sd2[:]).then_inc(eV, 1)
            V.tensor_scalar(out=sco[:], in0=sco[:], scalar1=sst[:, 1:2],
                            scalar2=None, op0=ALU.is_ge).then_inc(eV, 1)
            V.tensor_mul(swt[:], swt[:], sco[:]).then_inc(eV, 1)
            V.tensor_scalar(out=swt[:], in0=swt[:], scalar1=sd2[:, 0:1],
                            scalar2=None, op0=ALU.mult).then_inc(eV, 1)
            wv_ = eV.n
            bank_wait(T, [4])
            T.wait(eV, wv_)
            T.transpose(out=pBig[0:NE, 4 * 512:4 * 512 + 128], in_=swt[:],
                        identity=idf_sb[:]).then_inc(eT, 1)
            S.wait(eT)
            S.copy(out=wtf[:, tc * 128:(tc + 1) * 128],
                   in_=pBig[0:NE, 4 * 512:4 * 512 + 128]).then_inc(eS, 1)
            bank_used([4], eS, eS.n)
        wts_dma = sg_wt.issue(G, 0, g_wts[:], wtf[:], gate=(eS, eS.n))
        for e in range(NE):
            wrep_ld = st_wr.issue(Y, 0, wrep[:, e, :], bcast(g_wts[e:e + 1, :]),
                                  gate=wts_dma if e == 0 else None)
        w13_free = [wu_free[0], wu_free[1], wu_free[0], wu_free[1]]
        for e in range(NE):
            for fc in range(4):
                s1i, s3i = (fc % 2), 2 + (fc % 2)
                wls = []
                for wm, sbuf_i in [(w1, s1i), (w3, s3i)]:
                    wls.append(sg_13.issue(
                        G, sbuf_i, w13b[sbuf_i][:],
                        wm[e, :, fc * 128:(fc + 1) * 128]
                        .rearrange("(kc p) j -> p kc j", p=128),
                        gate=w13_free[sbuf_i]))
                bks = [0, 1, 2, 3] if fc % 2 == 0 else [4, 5, 6, 7]
                T.wait(*wls[0])
                T.wait(*wls[1])
                bank_wait(T, bks)
                for kc in range(16):
                    for th in range(2):
                        T.matmul(out=pb(bks[0] + th), lhsT=w13b[s1i][:, kc, :],
                                 rhs=xn_buf[:, kc, th * 512:(th + 1) * 512],
                                 start=(kc == 0), stop=(kc == 15))
                for kc in range(16):
                    for th in range(2):
                        mm = T.matmul(out=pb(bks[2] + th), lhsT=w13b[s3i][:, kc, :],
                                      rhs=xn_buf[:, kc, th * 512:(th + 1) * 512],
                                      start=(kc == 0), stop=(kc == 15))
                mm.then_inc(eT, 1)
                mt = eT.n
                w13_free[s1i] = (eT, mt)
                w13_free[s3i] = (eT, mt)
                S.wait(eT, mt)
                if e + fc > 0:
                    S.wait(eV)
                S.activation(out=m_t1[:], in_=pb2(bks[0]), func=AF.Tanh,
                             scale=0.5).then_inc(eS, 1)
                ev = eS.n
                V.wait(eS, ev)
                V.wait(*wrep_ld)
                V.scalar_tensor_tensor(out=m_g1s[:], in0=m_t1[:], scalar=1.0,
                                       in1=pb2(bks[0]), op0=ALU.add, op1=ALU.mult
                                       ).then_inc(eV, 1)
                V.tensor_mul(m_g1s[:], m_g1s[:], pb2(bks[2])).then_inc(eV, 1)
                V.tensor_mul(he_sb[:, e * 4 + fc, :], m_g1s[:], wrep[:, e, :]
                             ).then_inc(eV, 1)
                bank_used(bks, eV, eV.n)
        he_done = eV.n
        w2_free = [(None, 0), (None, 0)]
        T.wait(eV, he_done)
        for o in range(16):
            bks = [0, 1] if o % 2 == 0 else [2, 3]
            bank_wait(T, bks)
            first = True
            for e in range(NE):
                sl = e % 2
                wl = sg_w2.issue(G, sl, w2b[sl][:],
                                 w2[e, :, o * 128:(o + 1) * 128]
                                 .rearrange("(fc p) j -> p fc j", p=128),
                                 gate=w2_free[sl])
                T.wait(*wl)
                for fc in range(4):
                    for th in range(2):
                        mm = T.matmul(out=pb(bks[0] + th), lhsT=w2b[sl][:, fc, :],
                                      rhs=he_sb[:, e * 4 + fc, th * 512:(th + 1) * 512],
                                      start=first, stop=(e == NE - 1 and fc == 3))
                    first = False
                mm.then_inc(eT, 1)
                w2_free[sl] = (eT, eT.n)
            mt = eT.n
            S.wait(eT, mt)
            if o > 0:
                S.wait(eV)
            S.activation(out=m_t1[:], in_=pb2(bks[0]), func=AF.Tanh, scale=0.5
                         ).then_inc(eS, 1)
            V.wait(eS)
            V.scalar_tensor_tensor(out=xn_buf[:, o, :], in0=m_t1[:], scalar=1.0,
                                   in1=pb2(bks[0]), op0=ALU.add, op1=ALU.mult
                                   ).then_inc(eV, 1)
            bank_used(bks, eV, eV.n)
        g1_done = eV.n

        G.wait(qC, ar_kv)
        kv_ld = sg_kv.issue(G, 0, kvbf[:],
                            g_kv_out[:].rearrange("p (h j) -> p h j", j=128))
        T.wait(*kv_ld)
        for h in range(16):
            bks = [0, 1] if h % 2 == 0 else [2, 3]
            bank_wait(T, bks)
            for th in range(2):
                mm = T.matmul(out=pb(bks[0] + th), lhsT=kvbf[:, h, :],
                              rhs=qfm[:, h, th * 512:(th + 1) * 512],
                              start=True, stop=True)
            mm.then_inc(eT, 1)
            mt = eT.n
            S.wait(eT, mt)
            S.copy(out=ao_sb[:, h, :], in_=pb2(bks[0])).then_inc(eS, 1)
            S.activation(out=xsqb[:], in_=ao_sb[:, h, :], func=AF.Square
                         ).then_inc(eS, 1)
            sq = eS.n
            bank_used(bks, eS, sq)
            T.wait(eS, sq)
            if h == 0:
                bank_wait(T, [4, 5, 6, 7])
            for th in range(2):
                T.matmul(out=pb(4 + th), lhsT=onesb[:],
                         rhs=ao_sb[:, h, th * 512:(th + 1) * 512],
                         start=(h == 0), stop=(h == 15))
                mm = T.matmul(out=pb(6 + th), lhsT=onesb[:],
                              rhs=xsqb[:, th * 512:(th + 1) * 512],
                              start=(h == 0), stop=(h == 15))
            mm.then_inc(eT, 1)
        V.wait(eT)
        V.tensor_scalar_mul(mean_rep[:], pb2(4), 1.0 / HID).then_inc(eV, 1)
        V.tensor_scalar_mul(e2_rep[:], pb2(6), 1.0 / HID).then_inc(eV, 1)
        V.tensor_mul(vtmp[:], mean_rep[:], mean_rep[:]).then_inc(eV, 1)
        V.tensor_sub(vtmp[:], e2_rep[:], vtmp[:]).then_inc(eV, 1)
        bank_used([4, 5, 6, 7], eV, eV.n)
        S.wait(eV)
        S.activation(out=vtmp[:], in_=vtmp[:], func=AF.Ln, bias=eps_sb[:, 0:1]).then_inc(eS, 1)
        S.activation(out=rstd_rep[:], in_=vtmp[:], func=AF.Exp, scale=-0.5
                     ).then_inc(eS, 1)
        rstd2 = eS.n
        fin7 = eV.n
        u1_free = [(None, 0), (None, 0)]
        Y.wait(*st_u1.tgt(0))
        Y.wait(eV, fin7)
        Y.wait(eV, g1_done)
        Y.wait(qC, ar_kv)
        for hc in range(16):
            sl = hc % 2
            ld = st_ul.issue(Y, sl, u1ld[sl][:], sp_u1[hc, :, :],
                             gate=u1_free[sl])
            V.wait(eS, rstd2)
            V.wait(*ld)
            V.tensor_sub(nrst[:], ao_sb[:, hc, :], mean_rep[:]).then_inc(eV, 1)
            V.tensor_mul(nrst[:], nrst[:], rstd_rep[:]).then_inc(eV, 1)
            V.tensor_scalar(out=g2st[:], in0=nrst[:], scalar1=lng_sb[:, hc:hc + 1],
                            scalar2=lnb_sb[:, hc:hc + 1], op0=ALU.mult, op1=ALU.add
                            ).then_inc(eV, 1)
            V.tensor_mul(gated[:, hc, :], g2st[:], u1ld[sl][:]).then_inc(eV, 1)
            u1_free[sl] = (eV, eV.n)
        gated_done = eV.n
        wo_free = [(None, 0), (None, 0)]
        T.wait(eV, gated_done)
        for o in range(16):
            sl = o % 2
            wl = sg_wo.issue(G, sl, wob[sl][:], wo[:, o * 128:(o + 1) * 128]
                             .rearrange("(kc p) j -> p kc j", p=128),
                             gate=wo_free[sl])
            bks = [0, 1] if sl == 0 else [2, 3]
            T.wait(*wl)
            bank_wait(T, bks)
            for kc in range(16):
                for th in range(2):
                    mm = T.matmul(out=pb(bks[0] + th), lhsT=wob[sl][:, kc, :],
                                  rhs=gated[:, kc, th * 512:(th + 1) * 512],
                                  start=(kc == 0), stop=(kc == 15))
            mm.then_inc(eT, 1)
            mt = eT.n
            wo_free[sl] = (eT, mt)
            V.wait(eT, mt)
            V.tensor_scalar(out=s1_sb[:, o, :], in0=pb2(bks[0]), scalar1=0.5,
                            scalar2=bo_sb[:, o:o + 1], op0=ALU.mult, op1=ALU.add
                            ).then_inc(eV, 1)
            bank_used(bks, eV, eV.n)
        s1_done = eV.n

        x1l_free = [(None, 0), (None, 0)]
        Y.wait(*st_x1.tgt(0))
        Y.wait(eV, s1_done)
        for hc in range(16):
            sl = hc % 2
            ld = st_xl.issue(Y, sl, x1l[sl][:], sp_x1[hc, :, :],
                             gate=x1l_free[sl])
            V.wait(*ld)
            if hc >= 2:
                V.wait(*st_o.tgt(sl))
            V.tensor_mul(outst[sl][:], s1_sb[:, hc, :], xn_buf[:, hc, :]
                         ).then_inc(eV, 1)
            V.scalar_tensor_tensor(out=outst[sl][:], in0=outst[sl][:], scalar=0.5,
                                   in1=x1l[sl][:], op0=ALU.mult, op1=ALU.add
                                   ).then_inc(eV, 1)
            ov = eV.n
            x1l_free[sl] = (eV, ov)
            st_o.issue(Y, sl, out_t[hc * 128:(hc + 1) * 128, :], outst[sl][:],
                       gate=(eV, ov))
        Y.wait(*st_o.tgt(0))
        Y.wait(*st_o.tgt(1))

        with allow_ici_all_to_all(), nc.Block() as block:
            @block.tensor
            def _(t):
                T.replay(t)

            @block.vector
            def _(v):
                V.replay(v, auto_drain=True)

            @block.scalar
            def _(s):
                S.replay(s, auto_drain=True)

            @block.gpsimd
            def _(g):
                G.replay(g, auto_drain=True)

            @block.sync
            def _(sy):
                Y.replay(sy)

    return nc


_NC_CACHE = None


def kernel(**inp):
    global _NC_CACHE
    f32 = np.float32

    def C(a):
        return np.ascontiguousarray(a)

    x = np.asarray(inp["L0_attn_hidden_states"], f32)[0]
    xT = C(x.T)
    ipwT = C(np.asarray(inp["L0_attn_in_proj_w"], f32).T)
    convw = np.asarray(inp["L0_attn_conv_w"], f32)[:, 0, :]
    convb = np.asarray(inp["L0_attn_conv_b"], f32)
    alog = np.asarray(inp["L0_attn_A_log"], f32)
    dparam = np.asarray(inp["L0_attn_D_param"], f32)
    dtb = np.asarray(inp["L0_attn_dt_proj_b"], f32)
    xpwT = C(np.asarray(inp["L0_attn_x_proj_w"], f32).T)
    dtwT = C(np.asarray(inp["L0_attn_dt_proj_w"], f32).T)
    opwT = C(np.asarray(inp["L0_attn_out_proj_w"], f32).T)
    gwT = C(np.asarray(inp["L0_ffn_gate_w"], np.int32).T)
    uwT = C(np.asarray(inp["L0_ffn_up_w"], np.int32).T)
    dwnT = C(np.asarray(inp["L0_ffn_down_w"], np.int32).T)
    gscv = np.asarray(inp["L0_ffn_gate_scale"], f32)
    uscv = np.asarray(inp["L0_ffn_up_scale"], f32)
    dscv = np.asarray(inp["L0_ffn_down_scale"], f32)
    rwT = C(np.asarray(inp["L1_ffn_router_w"], f32).T)

    def col(v, nchunk):
        return C(v.reshape(nchunk, 128).T)

    shards = []
    for c in range(NCORE):
        tsl = slice(c * TOK, (c + 1) * TOK)
        csl = slice(c * DL, (c + 1) * DL)

        def ch4(m):
            return C(m.reshape(4, 128, *m.shape[1:]).transpose(
                1, 0, *range(2, m.ndim + 1)))

        d = {
            "x_t": C(xT[:, tsl]),
            "ipw": C(np.concatenate([ipwT[:, csl],
                                     ipwT[:, 4096 + c * DL:4096 + (c + 1) * DL]], 1)),
            "convw": ch4(convw[csl]),
            "convb": C(convb[csl].reshape(4, 128).T),
            "alog": ch4(alog[csl]),
            "dparam": C(dparam[csl].reshape(4, 128).T),
            "dtbi": C(dtb[csl].reshape(4, 128).T),
            "xpw": ch4(xpwT[csl]),
            "dtw": C(dtwT[:, csl]),
            "opw": opwT,
            "gw": gwT, "uw": uwT, "dwn": dwnT,
            "gsc": col(gscv, 64), "usc": col(uscv, 64), "dsc": col(dscv, 16),
            "wqk": np.asarray(inp["L1_attn_Wqk"], f32),
            "wv": np.asarray(inp["L1_attn_Wv"], f32),
            "wu": np.asarray(inp["L1_attn_Wu"], f32),
            "wo": np.asarray(inp["L1_attn_Wo"], f32),
            "bqk": col(np.asarray(inp["L1_attn_bqk"], f32), 16),
            "bvr": np.asarray(inp["L1_attn_bv"], f32).reshape(1, HID),
            "bui": col(np.asarray(inp["L1_attn_bu"], f32), 16),
            "boi": col(np.asarray(inp["L1_attn_bo"], f32), 16),
            "lng": col(np.asarray(inp["L1_attn_ln_g"], f32), 16),
            "lnb": col(np.asarray(inp["L1_attn_ln_b"], f32), 16),
            "rw": rwT,
            "w1": np.asarray(inp["L1_ffn_w1"], f32),
            "w2": np.asarray(inp["L1_ffn_w2"], f32),
            "w3": np.asarray(inp["L1_ffn_w3"], f32),
            "idf": np.eye(128, dtype=f32),
        }
        shards.append(d)

    nc = build_nc()
    res = run_bass_kernel_spmd(nc, shards, core_ids=list(range(NCORE)))
    out = np.concatenate([r["out_t"] for r in res.results], axis=1)
    return C(out.T).reshape(1, GT, HID)


if __name__ == "__main__":
    n = build_nc()
    print("built ok")


# version 11 (final)
# speedup vs baseline: 1.0894x; 1.0894x over previous
import numpy as np
import ml_dtypes
import concourse.bass as bass
import concourse.mybir as mybir
from concourse.bass_types import AP
from concourse.bass_utils import run_bass_kernel_spmd
from concourse.aot_env import allow_ici_all_to_all
from contextlib import ExitStack

F32 = mybir.dt.float32
BF16 = mybir.dt.bfloat16
I32 = mybir.dt.int32
I8 = mybir.dt.int8
AF = mybir.ActivationFunctionType
ALU = mybir.AluOpType

NCORE = 8
TOK = 1024
GT = 8192
HID = 2048
DL = 512
NST = 16
FFI = 8192
NE, FF = 8, 512
TB = 512
NB = GT // TB
RG = [list(range(NCORE))]
EPS = 1e-5


class Sem:
    def __init__(self, name):
        self.name, self.n, self.h = name, 0, None


class _Chain:
    def __init__(self, e):
        self.e = e

    def then_inc(self, sem, k=1):
        self.e[3].append((sem, k))
        sem.n += k
        return self


class Rec:
    def __init__(self, name):
        self.name, self.calls = name, []

    def wait(self, sem, tgt=None):
        t = sem.n if tgt is None else tgt
        if t > 0:
            self.calls.append(("wait_ge", (sem, t), {}, []))

    def op(self, meth, *a, **kw):
        e = [meth, a, kw, []]
        self.calls.append(e)
        return _Chain(e)

    def __getattr__(self, meth):
        if meth.startswith("_") or meth in ("name", "calls"):
            raise AttributeError(meth)

        def call(*a, **kw):
            return self.op(meth, *a, **kw)

        return call

    NO_DRAIN = {"wait_ge", "dma_start", "collective_compute", "memset"}

    def replay(self, eng, auto_drain=False):
        for meth, a, kw, incs in self.calls:
            if meth == "wait_ge":
                eng.wait_ge(a[0].h, a[1])
                continue
            inst = getattr(eng, meth)(*a, **kw)
            for sem, k in incs:
                inst.then_inc(sem.h, k)
            if auto_drain and meth not in Rec.NO_DRAIN:
                eng.drain()


class DStream:

    _all = []

    def __init__(self, name, nslots=1):
        self.name = name
        self.slots = [Sem(f"{name}{i}") for i in range(nslots)]
        DStream._all.append(self)

    def issue(self, rec, slot, out, in_, gate=None):
        if gate is not None and gate[0] is not None:
            rec.wait(gate[0], gate[1])
        rec.dma_start(out=out, in_=in_).then_inc(self.slots[slot], 16)
        return (self.slots[slot], self.slots[slot].n)

    def tgt(self, slot=0):
        return (self.slots[slot], self.slots[slot].n)


def splice_to(dst, tmp, state, upto):
    while state[0] < upto:
        dst.calls.append(tmp.calls[state[0]])
        state[0] += 1


def splice(dst, tmp, state, frac):
    total = state[1]
    want = min(total, int(np.ceil(total * frac)))
    while state[0] < want:
        dst.calls.append(tmp.calls[state[0]])
        state[0] += 1


def build_nc():
    nc = bass.Bass("TRN2", target_bir_lowering=False)

    def din(name, shape, dt=F32):
        return nc.dram_tensor(name, shape, dt, kind="ExternalInput").ap()

    x_t = din("x_t", [HID, TOK])
    ipw = din("ipw", [HID, 2 * DL])
    convw = din("convw", [128, 4, 4])
    convb = din("convb", [128, 4])
    alog = din("alog", [128, 4, NST])
    dparam = din("dparam", [128, 4])
    dtbi = din("dtbi", [128, 4])
    xpw = din("xpw", [128, 4, 160])
    dtw = din("dtw", [128, DL])
    opw = din("opw", [4096, HID])
    gw = din("gw", [HID, FFI], I32)
    uw = din("uw", [HID, FFI], I32)
    dwn = din("dwn", [FFI, HID], I32)
    gsc = din("gsc", [128, 64])
    usc = din("usc", [128, 64])
    dsc = din("dsc", [128, 16])
    wqk = din("wqk", [HID, HID])
    wv = din("wv", [HID, HID])
    wu = din("wu", [HID, HID])
    wo = din("wo", [HID, HID])
    bqk = din("bqk", [128, 16])
    bvr = din("bvr", [1, HID])
    bui = din("bui", [128, 16])
    boi = din("boi", [128, 16])
    lng = din("lng", [128, 16])
    lnb = din("lnb", [128, 16])
    rw = din("rw", [HID, NE])
    w1 = din("w1", [NE, HID, FF])
    w2 = din("w2", [NE, FF, HID])
    w3 = din("w3", [NE, HID, FF])
    idf = din("idf", [128, 128])

    out_t = nc.dram_tensor("out_t", [HID, TOK], F32, kind="ExternalOutput").ap()

    def dint(name, shape, dt):
        return nc.dram_tensor(name, shape, dt).ap()

    g_xn_in = dint("g_xn_in", [128, 16, TOK], BF16)
    g_xn_out = dint("g_xn_out", [NCORE, 128, 16, TOK], BF16)
    g_ssm_in = dint("g_ssm_in", [160, GT], F32)
    g_ssm_out = dint("g_ssm_out", [160, GT], F32)
    g_bc = dint("g_bc", [32, GT], BF16)
    g_y_in = dint("g_y_in", [NCORE, 128, 4, TOK], BF16)
    g_y_out = dint("g_y_out", [NCORE, 128, 4, TOK], BF16)
    g_kv_in = dint("g_kv_in", [128, HID], F32)
    g_kv_out = dint("g_kv_out", [128, HID], F32)
    g_wts = dint("g_wts", [NE, TOK], BF16)
    sp_delta = dint("sp_delta", [128, 4, GT], BF16)
    sp_u = dint("sp_u", [128, 4, GT], BF16)
    sp_gate = dint("sp_gate", [128, 4, GT], BF16)
    sp_h = dint("sp_h", [64, 128, TOK], BF16)
    sp_g0 = dint("sp_g0", [16, 128, TOK], BF16)
    sp_x1 = dint("sp_x1", [16, 128, TOK], F32)
    sp_u1 = dint("sp_u1", [16, 128, TOK], BF16)

    ctx = ExitStack()
    with ctx:
        AREN_A, AREN_B = 65536, 81920
        _arena = nc.alloc_sbuf_tensor("arena", [128, AREN_A + AREN_B], I8)
        _AB = nc.lookup_mloc(_arena).addr
        BB = AREN_A

        def at(name, shape, dt, off):
            return nc.alloc_sbuf_tensor_at(name, shape, dt, offset=_AB + off)

        ipw_sb = at("ipw_sb", [128, 16, 2 * DL], BF16, 0)
        xng = [at(f"xng{i}", [128, 16, 512], BF16, 32768 + i * 16384) for i in range(2)]
        sc_de = [at(f"sc_de{i}", [128, 4, TB], BF16, i * 4096) for i in range(2)]
        sc_u = [at(f"sc_u{i}", [128, 4, TB], BF16, 8192 + i * 4096) for i in range(2)]
        sc_ga = at("sc_ga", [128, 4, TB], BF16, 16384)
        sc_du = at("sc_du", [128, 4, TB], BF16, 20480)
        brep = [at(f"brep{i}", [128, 4, TB], BF16, 24576 + i * 4096) for i in range(2)]
        crep = [at(f"crep{i}", [128, 4, TB], BF16, 32768 + i * 4096) for i in range(2)]
        sc_hC = at("sc_hC", [128, 4, TB], BF16, 40960)
        sc_dA = [at(f"sc_dA{i}", [128, TB], BF16,
                    (45056 + i * 1024) if i < 2 else (53248 + (i - 2) * 1024))
                 for i in range(8)]
        sc_b = [at(f"sc_b{i}", [128, TB], BF16, 47104 + i * 1024) for i in range(2)]
        sc_h = [at(f"sc_h{i}", [128, TB], BF16,
                   (49152 + i * 1024) if i < 2 else (59392 + (i - 2) * 1024))
                for i in range(8)]
        sc_yf = [at(f"sc_yf{i}", [128, TB], BF16, 51200 + i * 1024) for i in range(2)]
        y_sb = at("y_sb", [128, 32, TOK], BF16, 0)
        qtok = at("qtok", [128, 8, 16, 128], BF16, 0)
        vtok = at("vtok", [128, 8, HID], BF16, 32768)
        he_sb = at("he_sb", [128, NE * 4, TOK], BF16, 0)
        ao_sb = at("ao_sb", [128, 16, TOK], BF16, 0)
        gated = at("gated", [128, 16, TOK], BF16, 32768)
        x1l = [at(f"x1l{i}", [128, TOK], F32, i * 4096) for i in range(2)]
        outst = [at(f"outst{i}", [128, TOK], F32, 8192 + i * 4096) for i in range(2)]

        xch = [at(f"xch{i}", [128, TOK], F32, BB + i * 4096) for i in range(2)]
        xsqf = [at(f"xsqf{i}", [128, TOK], F32, BB + 8192 + i * 4096) for i in range(2)]
        mean_rep = at("mean_rep", [128, TOK], F32, BB + 16384)
        rstd_rep = at("rstd_rep", [128, TOK], F32, BB + 20480)
        e2_rep = at("e2_rep", [128, TOK], F32, BB + 24576)
        xcbuf = at("xcbuf", [128, 4, 516], F32, BB + 0)
        ubuf = [at(f"ubuf{i}", [128, 4, 512], BF16, BB + 8256 + i * 4096)
                for i in range(2)]
        zst = at("zst", [128, 512], F32, BB + 45056)
        tzst = at("tzst", [128, 512], BF16, BB + 47104)
        gstg = [at(f"gstg{i}", [128, 512], BF16, BB + 48128 + i * 1024) for i in range(2)]
        bcst = [at(f"bcst{i}", [32, 512], F32, BB + 50176 + i * 2048) for i in range(2)]
        dstg = [at(f"dstg{i}", [128, 512], F32, BB + 54272 + i * 2048) for i in range(2)]
        dstg2 = [at(f"dstg2{i}", [128, 512], BF16, BB + 58368 + i * 1024) for i in range(2)]
        dtlr_sb = at("dtlr_sb", [128, GT], BF16, BB + 28672)
        gwb = [at(f"gwb{i}", [128, 16, 128], BF16, BB + 45056 + i * 4096) for i in range(2)]
        uwb = [at(f"uwb{i}", [128, 16, 128], BF16, BB + 53248 + i * 4096) for i in range(2)]
        f_tg = [at(f"f_tg{i}", [128, TOK], BF16, BB + 61440 + i * 2048) for i in range(2)]
        f_gv = [at(f"f_gv{i}", [128, TOK], BF16, BB + 18432 + i * 2048) for i in range(2)]
        f_uv = [at(f"f_uv{i}", [128, TOK], BF16, BB + 22528 + i * 2048) for i in range(2)]
        f_h = [at(f"f_h{i}", [128, TOK], BF16, BB + 67584 + i * 2048) for i in range(2)]
        dwb = [at(f"dwb{i}", [128, 512], BF16, BB + 71680 + i * 1024) for i in range(2)]
        d_hi = [at(f"d_hi{i}", [128, TOK], BF16, BB + 0 + i * 2048) for i in range(2)]
        d_tf = at("d_tf", [128, TOK], BF16, BB + 4096)
        d_fv = at("d_fv", [128, TOK], BF16, BB + 6144)
        d_g0 = [at(f"d_g0{i}", [128, TOK], BF16, BB + 8192 + i * 2048) for i in range(2)]
        sc_ya = at("sc_ya", [128, 4, TB], F32, BB + 73728)
        sc_yt = at("sc_yt", [128, TB], F32, BB + 12288)
        sc_yp = at("sc_yp", [128, TB], F32, BB + 14336)
        sc_tg = at("sc_tg", [128, TB], BF16, BB + 16384)
        sc_gr = at("sc_gr", [128, TB], BF16, BB + 17408)
        opwb = [at(f"opwb{i}", [128, 32, 128], BF16, BB + 28672 + i * 8192) for i in range(2)]
        x1b = at("x1b", [128, 16, TOK], BF16, BB + 45056)
        g0ld = [at(f"g0ld{i}", [128, TOK], BF16, BB + 0 + i * 2048) for i in range(2)]
        xrld = [at(f"xrld{i}", [128, TOK], F32, BB + 4096 + i * 4096) for i in range(2)]
        x1st = at("x1st", [128, TOK], F32, BB + 77824)
        qf32 = [at(f"qf32{i}", [128, TOK], F32, BB + 28672 + i * 4096) for i in range(2)]
        elu_e = at("elu_e", [128, TOK], F32, BB + 36864)
        elu_m = at("elu_m", [128, TOK], F32, BB + 40960)
        wqkb = [at(f"wqkb{i}", [128, 16, 128], BF16, BB + 0 + i * 4096) for i in range(2)]
        tstg = [at(f"tstg{i}", [128, 128], BF16, BB + 8192 + i * 256) for i in range(2)]
        vstg = at("vstg", [128, 512], BF16, BB + 8704)
        kv_sb = at("kv_sb", [128, HID], F32, BB + 16384)
        qfm = at("qfm", [128, 16, TOK], BF16, BB + 49152)
        w13b = [at(f"w13b{i}", [128, 16, 128], BF16, BB + 0 + i * 4096) for i in range(4)]
        w2b = [at(f"w2b{i}", [128, 4, 128], BF16, BB + 16384 + i * 1024) for i in range(2)]
        m_t1 = at("m_t1", [128, TOK], BF16, BB + 18432)
        m_g1s = at("m_g1s", [128, TOK], BF16, BB + 20480)
        wrep = at("wrep", [128, NE, TOK], BF16, BB + 22528)
        kvbf = at("kvbf", [128, 16, 128], BF16, BB + 28672)
        nrst = at("nrst", [128, TOK], F32, BB + 0)
        g2st = at("g2st", [128, TOK], F32, BB + 4096)
        wob = [at(f"wob{i}", [128, 16, 128], BF16, BB + 8192 + i * 4096) for i in range(2)]
        u1ld = [at(f"u1ld{i}", [128, TOK], BF16, BB + 24576 + i * 2048) for i in range(2)]
        s1_sb = at("s1_sb", [128, 16, TOK], BF16, BB + 49152)

        def sb(name, shape, dt):
            return nc.alloc_sbuf_tensor(name, shape, dt)

        onesf = sb("onesf", [128, 128], F32)
        ones1b = sb("ones1b", [1, 128], BF16)
        onesb = sb("onesb", [128, 128], BF16)
        idf_sb = sb("idf_sb", [128, 128], F32)
        convw_sb = sb("convw_sb", [128, 4, 4], F32)
        convb_sb = sb("convb_sb", [128, 4], F32)
        a_sb = sb("a_sb", [128, 4, NST], F32)
        dprm_sb = sb("dprm_sb", [128, 4], F32)
        dtb_sb = sb("dtb_sb", [128, 4], F32)
        gsc_sb = sb("gsc_sb", [128, 64], F32)
        gsch_sb = sb("gsch_sb", [128, 64], F32)
        guh_sb = sb("guh_sb", [128, 64], F32)
        usch_sb = sb("usch_sb", [128, 64], F32)
        dsc_sb = sb("dsc_sb", [128, 16], F32)
        dsch_sb = sb("dsch_sb", [128, 16], F32)
        bqk_sb = sb("bqk_sb", [128, 16], F32)
        bu_sb = sb("bu_sb", [128, 16], F32)
        buh_sb = sb("buh_sb", [128, 16], F32)
        bo_sb = sb("bo_sb", [128, 16], F32)
        lng_sb = sb("lng_sb", [128, 16], F32)
        lnb_sb = sb("lnb_sb", [128, 16], F32)
        bvr_sb = sb("bvr_sb", [1, HID], BF16)
        dtw_sb = sb("dtw_sb", [128, DL], BF16)
        xpw_sb = sb("xpw_sb", [128, 4, 160], BF16)
        hcar = sb("hcar", [128, 4, NST], F32)
        xtail = sb("xtail", [128, 4, 4], F32)
        xn_buf = sb("xn_buf", [128, 16, TOK], BF16)
        u1st = sb("u1st", [128, TOK], BF16)
        vtmp = sb("vtmp", [128, TOK], F32)
        xsqb = sb("xsqb", [128, TOK], BF16)
        rwb = sb("rwb", [128, 16, NE], BF16)
        sco = sb("sco", [128, NE], F32)
        sst = sb("sst", [128, NE], F32)
        swt = sb("swt", [128, NE], F32)
        sd1 = sb("sd1", [128, 1], F32)
        sd2 = sb("sd2", [128, 1], F32)
        wtf = sb("wtf", [8, TOK], F32)
        eps_sb = sb("eps_sb", [128, 1], F32)
        m128_sb = sb("m128_sb", [128, 1], F32)

        pBig = ctx.enter_context(nc.psum_tensor("pBig", [128, 4096], F32))

        def pb(bank, n=512):
            return pBig[:, bank * 512: bank * 512 + n]

        def pb2(bank):
            return pBig[:, bank * 512: bank * 512 + 1024]

        sems = [Sem(s) for s in ["eT", "eV", "eV2", "eS", "eS2", "qC", "eG"]]
        eT, eV, eV2, eS, eS2, qC, eG = sems
        DStream._all = []
        st_x = DStream("st_x", 2)
        st_ng = DStream("st_ng", 2)
        st_gs = DStream("st_gs", 2)
        st_us = DStream("st_us", 2)
        st_ss = DStream("st_ss", 2)
        st_ds = DStream("st_ds", 2)
        st_sb = DStream("st_sb", 2)
        st_bc = DStream("st_bc", 2)
        st_ga = DStream("st_ga", 1)
        st_yf = DStream("st_yf", 2)
        st_h = DStream("st_h", 2)
        st_hl = DStream("st_hl", 2)
        st_g0 = DStream("st_g0", 2)
        st_ysb = DStream("st_ysb", 1)
        st_gl = DStream("st_gl", 2)
        st_xr = DStream("st_xr", 2)
        st_x1 = DStream("st_x1", 1)
        st_xl = DStream("st_xl", 2)
        st_u1 = DStream("st_u1", 1)
        st_ul = DStream("st_ul", 2)
        st_kv = DStream("st_kv", 1)
        st_wr = DStream("st_wr", 1)
        st_o = DStream("st_o", 2)
        st_xn = DStream("st_xn", 1)
        sg_sm = DStream("sg_sm", 1)
        sg_dt = DStream("sg_dt", 1)
        sg_w = DStream("sg_w", 2)
        sg_d = DStream("sg_d", 2)
        sg_op = DStream("sg_op", 2)
        sg_q = DStream("sg_q", 2)
        sg_13 = DStream("sg_13", 4)
        sg_w2 = DStream("sg_w2", 2)
        sg_kv = DStream("sg_kv", 1)
        sg_wt = DStream("sg_wt", 1)
        sg_wo = DStream("sg_wo", 2)
        for s in sems:
            s.h = ctx.enter_context(nc.semaphore(s.name))
        for ds_ in DStream._all:
            for s in ds_.slots:
                s.h = ctx.enter_context(nc.semaphore(s.name))

        T, V, S, G, Y = Rec("T"), Rec("V"), Rec("S"), Rec("G"), Rec("Y")
        S2, V2, Y2 = Rec("S2"), Rec("V2"), Rec("Y2")

        bank_last = [None] * 8

        def bank_wait(rec, banks):
            for b in banks:
                if bank_last[b] is not None:
                    rec.wait(*bank_last[b])

        def bank_used(banks, sem, tgt):
            for b in banks:
                bank_last[b] = (sem, tgt)

        def bcast(dram_row):
            return AP(dram_row.tensor, dram_row.offset,
                      [[0, 128]] + list(dram_row.ap)[1:])

        for src, dst in [(convw, convw_sb), (convb, convb_sb), (alog, a_sb),
                         (dparam, dprm_sb), (dtbi, dtb_sb), (gsc, gsc_sb),
                         (dsc, dsc_sb), (bqk, bqk_sb), (bui, bu_sb),
                         (boi, bo_sb), (lng, lng_sb), (lnb, lnb_sb),
                         (idf, idf_sb), (bvr, bvr_sb), (dtw, dtw_sb),
                         (xpw, xpw_sb), (usc, usch_sb)]:
            sg_sm.issue(G, 0, dst[:], src[:])
        sg_sm.issue(G, 0, rwb[:], rw[:].rearrange("(kc p) e -> p kc e", p=128))
        sg_sm.issue(G, 0, ipw_sb[:], ipw[:].rearrange("(kc p) j -> p kc j", p=128))
        smalls = sg_sm.tgt()

        V.wait(*smalls)
        for t_, v_ in [(onesf, 1.0), (onesb, 1.0), (ones1b, 1.0),
                       (hcar, 0.0), (xtail, 0.0), (eps_sb, EPS),
                       (m128_sb, -128.0)]:
            V.memset(t_[:], v_).then_inc(eV, 1)
        V.tensor_scalar_mul(convw_sb[:], convw_sb[:], 0.5).then_inc(eV, 1)
        V.tensor_scalar_mul(convb_sb[:], convb_sb[:], 0.5).then_inc(eV, 1)
        V.tensor_scalar_mul(gsch_sb[:], gsc_sb[:], 0.5).then_inc(eV, 1)
        V.tensor_scalar_mul(usch_sb[:], usch_sb[:], 0.5).then_inc(eV, 1)
        V.tensor_scalar_mul(dsch_sb[:], dsc_sb[:], 0.5).then_inc(eV, 1)
        V.tensor_scalar_mul(buh_sb[:], bu_sb[:], 0.5).then_inc(eV, 1)
        V.tensor_mul(guh_sb[:], gsc_sb[:], usch_sb[:]).then_inc(eV, 1)
        S.wait(*smalls)
        S.activation(out=a_sb[:], in_=a_sb[:], func=AF.Exp).then_inc(eS, 1)
        V.wait(eS)
        V.tensor_scalar_mul(a_sb[:], a_sb[:], -1.0).then_inc(eV, 1)
        prep_eV = eV.n

        xch_free = [(None, 0), (None, 0)]
        for hc in range(16):
            sl = xch[hc % 2]
            ld = st_x.issue(Y, hc % 2, sl[:], x_t[hc * 128:(hc + 1) * 128, :],
                            gate=xch_free[hc % 2])
            S.wait(*ld)
            S.activation(out=xsqf[hc % 2][:], in_=sl[:], func=AF.Square
                         ).then_inc(eS, 1)
            sq = eS.n
            T.wait(*ld)
            T.wait(eS, sq)
            if hc == 0:
                T.wait(eV, prep_eV)
                bank_wait(T, [0, 1, 2, 3])
            for th in range(2):
                T.matmul(out=pb(0 + th), lhsT=onesf[:],
                         rhs=sl[:, th * 512:(th + 1) * 512],
                         start=(hc == 0), stop=(hc == 15))
                mm = T.matmul(out=pb(2 + th), lhsT=onesf[:],
                              rhs=xsqf[hc % 2][:, th * 512:(th + 1) * 512],
                              start=(hc == 0), stop=(hc == 15))
            mm.then_inc(eT, 1)
            xch_free[hc % 2] = (eT, eT.n)
        V.wait(eT)
        V.tensor_scalar_mul(mean_rep[:], pb2(0), 1.0 / HID).then_inc(eV, 1)
        V.tensor_scalar_mul(e2_rep[:], pb2(2), 1.0 / HID).then_inc(eV, 1)
        V.tensor_mul(vtmp[:], mean_rep[:], mean_rep[:]).then_inc(eV, 1)
        V.tensor_sub(vtmp[:], e2_rep[:], vtmp[:]).then_inc(eV, 1)
        bank_used([0, 1, 2, 3], eV, eV.n)
        S.wait(eV)
        S.activation(out=vtmp[:], in_=vtmp[:], func=AF.Ln, bias=eps_sb[:, 0:1]).then_inc(eS, 1)
        S.activation(out=rstd_rep[:], in_=vtmp[:], func=AF.Exp, scale=-0.5
                     ).then_inc(eS, 1)
        rstd0 = eS.n
        for hc in range(16):
            sl = xch[hc % 2]
            gate = xch_free[hc % 2] if hc < 2 else (eV, eV.n)
            ld = st_x.issue(Y, hc % 2, sl[:], x_t[hc * 128:(hc + 1) * 128, :],
                            gate=gate)
            V.wait(*ld)
            V.wait(eS, rstd0)
            V.tensor_sub(vtmp[:], sl[:], mean_rep[:]).then_inc(eV, 1)
            V.tensor_mul(xn_buf[:, hc, :], vtmp[:], rstd_rep[:]).then_inc(eV, 1)
        xn0_done = eV.n
        xn_st = st_xn.issue(Y, 0, g_xn_in[:], xn_buf[:], gate=(eV, xn0_done))
        G.wait(*xn_st)
        G.collective_compute("AllGather", ALU.bypass, replica_groups=RG,
                             ins=[g_xn_in], outs=[g_xn_out]).then_inc(qC, 1)
        ag_xn = qC.n

        T.wait(qC, ag_xn)
        Y.wait(qC, ag_xn)
        xng_free = [(None, 0), (None, 0)]
        gs_prev = [(None, 0), (None, 0)]
        ss_prev = [(None, 0), (None, 0)]
        usp_prev = [(None, 0), (None, 0)]
        uv_tgt = [0, 0]
        xproj_eT = [0, 0]
        for tt in range(16):
            buf = xng[tt % 2]
            r, lo = tt // 2, (tt % 2) * 512
            ld = st_ng.issue(Y, tt % 2, buf[:], g_xn_out[r, :, :, lo:lo + 512],
                             gate=xng_free[tt % 2])
            T.wait(*ld)
            for oc in range(8):
                bk = oc % 4
                bank_wait(T, [bk])
                for kc in range(16):
                    mm = T.matmul(out=pb(bk),
                                  lhsT=ipw_sb[:, kc, oc * 128:(oc + 1) * 128],
                                  rhs=buf[:, kc, :], start=(kc == 0), stop=(kc == 15))
                mm.then_inc(eT, 1)
                mt = eT.n
                if oc < 4:
                    cc = oc
                    S.wait(eT, mt)
                    S.copy(out=xcbuf[:, cc, 4:516], in_=pb(bk)).then_inc(eS, 1)
                    bank_used([bk], eS, eS.n)
                    V.wait(eS)
                    if tt > 1 and cc == 0:
                        V.wait(usp_prev[tt % 2][0], usp_prev[tt % 2][1])
                        V.wait(eT, xproj_eT[tt % 2])
                    V.tensor_copy(out=xcbuf[:, cc, 1:4], in_=xtail[:, cc, 1:4]
                                  ).then_inc(eV, 1)
                    V.tensor_scalar(out=zst[:], in0=xcbuf[:, cc, 1:513],
                                    scalar1=convw_sb[:, cc, 0:1],
                                    scalar2=convb_sb[:, cc:cc + 1],
                                    op0=ALU.mult, op1=ALU.add).then_inc(eV, 1)
                    for k in range(1, 4):
                        V.scalar_tensor_tensor(out=zst[:], in0=xcbuf[:, cc, 1 + k:513 + k],
                                               scalar=convw_sb[:, cc, k:k + 1], in1=zst[:],
                                               op0=ALU.mult, op1=ALU.add).then_inc(eV, 1)
                    V.tensor_copy(out=xtail[:, cc, 1:4], in_=xcbuf[:, cc, 513:516]
                                  ).then_inc(eV, 1)
                    S.wait(eV)
                    S.activation(out=tzst[:], in_=zst[:], func=AF.Tanh).then_inc(eS, 1)
                    V.wait(eS)
                    V.scalar_tensor_tensor(out=ubuf[tt % 2][:, cc, :], in0=tzst[:],
                                           scalar=1.0, in1=zst[:],
                                           op0=ALU.add, op1=ALU.mult
                                           ).then_inc(eV, 1)
                else:
                    gg = gstg[oc % 2]
                    if gs_prev[oc % 2][0] is not None:
                        S.wait(*gs_prev[oc % 2])
                    S.wait(eT, mt)
                    S.copy(out=gg[:], in_=pb(bk)).then_inc(eS, 1)
                    bank_used([bk], eS, eS.n)
                    gs_prev[oc % 2] = st_gs.issue(
                        Y, oc % 2, sp_gate[:, oc - 4, tt * 512:(tt + 1) * 512],
                        gg[:], gate=(eS, eS.n))
            xng_free[tt % 2] = (eT, eT.n)
            uv_tgt[tt % 2] = eV.n
            usp_prev[tt % 2] = st_us.issue(
                Y, tt % 2, sp_u[:, :, tt * 512:(tt + 1) * 512],
                ubuf[tt % 2][:], gate=(eV, eV.n))
            for lt in ([tt - 1] if tt > 0 else []) + ([tt] if tt == 15 else []):
                sl_ = lt % 2
                bank_wait(T, [4, 5])
                T.wait(eV, uv_tgt[sl_])
                for cc in range(4):
                    T.matmul(out=pb(4), lhsT=xpw_sb[:, cc, 0:128],
                             rhs=ubuf[sl_][:, cc, :],
                             start=(cc == 0), stop=(cc == 3))
                    mm = T.matmul(out=pBig[0:32, 5 * 512:5 * 512 + 512],
                                  lhsT=xpw_sb[:, cc, 128:160],
                                  rhs=ubuf[sl_][:, cc, :],
                                  start=(cc == 0), stop=(cc == 3))
                mm.then_inc(eT, 1)
                xproj_eT[sl_] = eT.n
                if ss_prev[sl_][0] is not None:
                    S.wait(*ss_prev[sl_])
                S.wait(eT)
                S.copy(out=dstg[sl_][:], in_=pb(4)).then_inc(eS, 1)
                S.copy(out=bcst[sl_][:], in_=pBig[0:32, 5 * 512:5 * 512 + 512]
                       ).then_inc(eS, 1)
                bank_used([4, 5], eS, eS.n)
                st_ss.issue(Y, sl_, g_ssm_in[0:128, lt * 512:(lt + 1) * 512],
                            dstg[sl_][:], gate=(eS, eS.n))
                ss_prev[sl_] = st_ss.issue(
                    Y, sl_, g_ssm_in[128:160, lt * 512:(lt + 1) * 512],
                    bcst[sl_][:])
        G.wait(*st_ss.tgt(0))
        G.wait(*st_ss.tgt(1))
        G.wait(*st_us.tgt(0))
        G.wait(*st_gs.tgt(0))
        G.wait(*st_gs.tgt(1))
        G.collective_compute("AllReduce", ALU.add, replica_groups=RG,
                             ins=[g_ssm_in], outs=[g_ssm_out]).then_inc(qC, 1)
        ar_ssm = qC.n
        G.wait(qC, ar_ssm)
        sg_dt.issue(G, 0, dtlr_sb[:], g_ssm_out[0:128, :])
        dtlr_ld = sg_dt.issue(G, 0, g_bc[:], g_ssm_out[128:160, :])

        T.wait(*dtlr_ld)
        S.wait(*st_ss.tgt(0))
        S.wait(*st_ss.tgt(1))
        ds_prev = [(None, 0), (None, 0)]
        ds_rd = [(None, 0), (None, 0)]
        for cc in range(4):
            for tq in range(16):
                bk = (cc * 16 + tq) % 4
                bank_wait(T, [bk])
                mm = T.matmul(out=pb(bk), lhsT=dtw_sb[:, cc * 128:(cc + 1) * 128],
                              rhs=dtlr_sb[:, tq * 512:(tq + 1) * 512],
                              start=True, stop=True)
                mm.then_inc(eT, 1)
                mt = eT.n
                st1 = dstg[tq % 2]
                st2 = dstg2[tq % 2]
                S.wait(eT, mt)
                if ds_rd[tq % 2][0] is not None:
                    S.wait(*ds_rd[tq % 2])
                S.activation(out=st1[:], in_=pb(bk), func=AF.Abs,
                             bias=dtb_sb[:, cc:cc + 1]).then_inc(eS, 1)
                S.activation(out=st1[:], in_=st1[:], func=AF.Exp, scale=-1.0
                             ).then_inc(eS, 1)
                S.activation(out=st1[:], in_=st1[:], func=AF.Ln, bias=1.0
                             ).then_inc(eS, 1)
                V.wait(eS)
                if ds_prev[tq % 2][0] is not None:
                    V.wait(*ds_prev[tq % 2])
                V.tensor_scalar(out=vtmp[:, 0:512], in0=pb(bk),
                                scalar1=dtb_sb[:, cc:cc + 1], scalar2=0.0,
                                op0=ALU.add, op1=ALU.max).then_inc(eV, 1)
                V.tensor_add(st2[:], vtmp[:, 0:512], st1[:]).then_inc(eV, 1)
                ds_rd[tq % 2] = (eV, eV.n)
                bank_used([bk], eV, eV.n)
                ds_prev[tq % 2] = st_ds.issue(
                    Y, tq % 2, sp_delta[:, cc, tq * 512:(tq + 1) * 512],
                    st2[:], gate=(eV, eV.n))

        gwb_free = [(None, 0), (None, 0)]
        hsp_prev = [(None, 0), (None, 0)]
        stg_rd = [(None, 0), (None, 0)]
        unit_marks = []
        G.wait(eV, eV.n)
        G.wait(eS, eS.n)
        G.wait(*st_ds.tgt(0))
        G.wait(*st_ds.tgt(1))
        for i in range(64):
            sl = i % 2
            sg_w.issue(G, sl, gwb[sl][:], gw[:, i * 128:(i + 1) * 128]
                       .rearrange("(kc p) j -> p kc j", p=128), gate=gwb_free[sl])
            wl = sg_w.issue(G, sl, uwb[sl][:], uw[:, i * 128:(i + 1) * 128]
                            .rearrange("(kc p) j -> p kc j", p=128))
            S2.wait(*wl)
            S2.op("add", gwb[sl][:], gwb[sl][:], m128_sb[:, 0:1]).then_inc(eS2, 1)
            S2.op("add", uwb[sl][:], uwb[sl][:], m128_sb[:, 0:1]).then_inc(eS2, 1)
            sh = eS2.n
            banks = [0, 1, 2, 3] if sl == 0 else [4, 5, 6, 7]
            T.wait(eS2, sh)
            bank_wait(T, banks)
            for th in range(2):
                for kc in range(16):
                    mm = T.matmul(out=pb(banks[0] + th),
                                  lhsT=gwb[sl][:, kc, :],
                                  rhs=xn_buf[:, kc, th * 512:(th + 1) * 512],
                                  start=(kc == 0), stop=(kc == 15))
            for th in range(2):
                for kc in range(16):
                    mm = T.matmul(out=pb(banks[2] + th),
                                  lhsT=uwb[sl][:, kc, :],
                                  rhs=xn_buf[:, kc, th * 512:(th + 1) * 512],
                                  start=(kc == 0), stop=(kc == 15))
            mm.then_inc(eT, 1)
            mt = eT.n

            gwb_free[sl] = (eT, mt)
            S2.wait(eT, mt)
            if stg_rd[sl][0] is not None:
                S2.wait(*stg_rd[sl])
            S2.activation(out=f_tg[sl][:], in_=pb2(banks[0]), func=AF.Tanh,
                          scale=gsch_sb[:, i:i + 1]).then_inc(eS2, 1)
            ev = eS2.n
            V2.wait(eS2, ev)
            if hsp_prev[sl][0] is not None:
                V2.wait(*hsp_prev[sl])
            V2.scalar_tensor_tensor(out=f_gv[sl][:], in0=f_tg[sl][:], scalar=1.0,
                                    in1=pb2(banks[0]), op0=ALU.add, op1=ALU.mult
                                    ).then_inc(eV2, 1)
            V2.tensor_mul(f_h[sl][:], f_gv[sl][:], pb2(banks[2])).then_inc(eV2, 1)
            V2.tensor_scalar(out=f_h[sl][:], in0=f_h[sl][:],
                             scalar1=guh_sb[:, i:i + 1], scalar2=None,
                             op0=ALU.mult).then_inc(eV2, 1)
            stg_rd[sl] = (eV2, eV2.n)
            bank_used(banks, eV2, eV2.n)
            hsp_prev[sl] = st_h.issue(Y2, sl, sp_h[i, :, :], f_h[sl][:],
                                      gate=(eV2, eV2.n))
            unit_marks.append((len(S2.calls), len(V2.calls), len(Y2.calls)))
        hspill_all = [st_h.tgt(0), st_h.tgt(1)]

        dwb_free = [(None, 0), (None, 0)]
        dhi_free = [(None, 0), (None, 0)]
        g0_prev = [(None, 0), (None, 0)]
        dstg_rd = (None, 0)
        for ob in range(4):
            for i in range(64):
                sl = i % 2
                wl = sg_d.issue(G, sl, dwb[sl][:],
                                dwn[i * 128:(i + 1) * 128, ob * 512:(ob + 1) * 512],
                                gate=dwb_free[sl])
                S2.wait(*wl)
                S2.op("add", dwb[sl][:], dwb[sl][:], m128_sb[:, 0:1]).then_inc(eS2, 1)
                sh = eS2.n
                if ob == 0 and i < 2:
                    Y2.wait(*hspill_all[0])
                    Y2.wait(*hspill_all[1])
                hl = st_hl.issue(Y2, sl, d_hi[sl][:], sp_h[i, :, :],
                                 gate=dhi_free[sl])
                T.wait(eS2, sh)
                T.wait(*hl)
                if i == 0:
                    bank_wait(T, [0, 1, 2, 3, 4, 5, 6, 7])
                for ol in range(4):
                    for th in range(2):
                        mm = T.matmul(out=pb(2 * ol + th),
                                      lhsT=dwb[sl][:, ol * 128:(ol + 1) * 128],
                                      rhs=d_hi[sl][:, th * 512:(th + 1) * 512],
                                      start=(i == 0), stop=(i == 63))
                mm.then_inc(eT, 1)
                dwb_free[sl] = (eT, eT.n)
                dhi_free[sl] = (eT, eT.n)
                unit_marks.append((len(S2.calls), len(V2.calls), len(Y2.calls)))
            mt = eT.n
            for ol in range(4):
                o = ob * 4 + ol
                S2.wait(eT, mt)
                if dstg_rd[0] is not None:
                    S2.wait(*dstg_rd)
                S2.activation(out=d_tf[:], in_=pb2(2 * ol), func=AF.Tanh,
                              scale=dsch_sb[:, o:o + 1]).then_inc(eS2, 1)
                ev = eS2.n
                V2.wait(eS2, ev)
                if g0_prev[ol % 2][0] is not None:
                    V2.wait(*g0_prev[ol % 2])
                V2.scalar_tensor_tensor(out=d_g0[ol % 2][:], in0=d_tf[:], scalar=1.0,
                                        in1=pb2(2 * ol), op0=ALU.add, op1=ALU.mult
                                        ).then_inc(eV2, 1)
                V2.tensor_scalar(out=d_g0[ol % 2][:], in0=d_g0[ol % 2][:],
                                 scalar1=dsc_sb[:, o:o + 1], scalar2=None,
                                 op0=ALU.mult).then_inc(eV2, 1)
                dstg_rd = (eV2, eV2.n)
                g0_prev[ol % 2] = st_g0.issue(Y2, ol % 2, sp_g0[o, :, :],
                                              d_g0[ol % 2][:], gate=(eV2, eV2.n))
                if ol == 3:
                    bank_used([0, 1, 2, 3, 4, 5, 6, 7], eV2, eV2.n)
            unit_marks.append((len(S2.calls), len(V2.calls), len(Y2.calls)))

        s2_state = [0, len(S2.calls)]
        v2_state = [0, len(V2.calls)]
        y2_state = [0, len(Y2.calls)]
        de_free = [(None, 0), (None, 0)]
        ga_free = (None, 0)
        rep_free = [(None, 0), (None, 0)]
        yf_prev = [(None, 0), (None, 0)]
        tg_rd = (None, 0)
        dA_free = [(None, 0)] * 8
        hq_pend = []
        hs_done = [None] * 8
        rot8 = 0
        for b in range(NB):
            sl = b % 2
            if b == 0:
                Y.wait(*st_ds.tgt(0))
                Y.wait(*st_ds.tgt(1))
                Y.wait(*st_us.tgt(0))
                Y.wait(*st_us.tgt(1))
            st_sb.issue(Y, sl, sc_de[sl][:], sp_delta[:, :, b * TB:(b + 1) * TB],
                        gate=de_free[sl])
            blk_ld = st_sb.issue(Y, sl, sc_u[sl][:], sp_u[:, :, b * TB:(b + 1) * TB])
            def issue_bcast(nq, gate):
                qsl = nq % 2
                for n4 in range(4):
                    n = nq * 4 + n4
                    st_bc.issue(Y, qsl, brep[qsl][:, n4, :],
                                bcast(g_bc[n:n + 1, b * TB:(b + 1) * TB]),
                                gate=gate if n4 == 0 else None)
                    st_bc.issue(Y, qsl, crep[qsl][:, n4, :],
                                bcast(g_bc[16 + n:17 + n, b * TB:(b + 1) * TB]))
                return st_bc.tgt(qsl)

            nq_tgt = [None] * 4
            for nq0 in range(2):
                gate0 = rep_free[nq0 % 2] if b > 0 else (
                    dtlr_ld if nq0 == 0 else (None, 0))
                nq_tgt[nq0] = issue_bcast(nq0, gate0)
            if b == 0:
                Y.wait(*st_gs.tgt(0))
                Y.wait(*st_gs.tgt(1))
            ga_ld = st_ga.issue(Y, 0, sc_ga[:], sp_gate[:, :, b * TB:(b + 1) * TB],
                                gate=ga_free)

            V.wait(*blk_ld)
            V.tensor_mul(sc_du[:], sc_de[sl][:], sc_u[sl][:]).then_inc(eV, 1)
            for nq in range(4):
                qsl = nq % 2
                V.wait(*nq_tgt[nq])
                for cc in range(4):
                    for n4 in range(4):
                        n = nq * 4 + n4
                        r = rot8 % 8
                        rh = rot8 % 8
                        rb = rot8 % 2
                        w = dA_free[r]
                        if w[0] is not None:
                            S.wait(w[0], w[1])
                        S.wait(*blk_ld)
                        S.activation(out=sc_dA[r][:], in_=sc_de[sl][:, cc, :],
                                     func=AF.Exp, scale=a_sb[:, cc, n:n + 1]
                                     ).then_inc(eS, 1)
                        da = eS.n
                        if len(hq_pend) >= 7:
                            erh, ecc, en, etts = hq_pend.pop(0)
                            S.wait(eV, etts)
                            S.activation(out=hcar[:, ecc, en:en + 1],
                                         in_=sc_h[erh][:, TB - 1:TB],
                                         func=AF.Copy).then_inc(eS, 1)
                            hs_done[erh] = eS.n
                        V.tensor_mul(sc_b[rb][:], sc_du[:, cc, :], brep[qsl][:, n4, :]
                                     ).then_inc(eV, 1)
                        V.wait(eS, da)
                        if hs_done[rh] is not None:
                            V.wait(eS, hs_done[rh])
                        V.tensor_tensor_scan(out=sc_h[rh][:], data0=sc_dA[r][:],
                                             data1=sc_b[rb][:],
                                             initial=hcar[:, cc, n:n + 1],
                                             op0=ALU.mult, op1=ALU.add
                                             ).then_inc(eV, 1)
                        if b < NB - 1:
                            hq_pend.append((rh, cc, n, eV.n))
                        V.tensor_mul(sc_hC[:, n4, :], sc_h[rh][:], crep[qsl][:, n4, :]
                                     ).then_inc(eV, 1)
                        dA_free[r] = (eV, eV.n)
                        rot8 += 1
                    if nq == 0:
                        V.tensor_reduce(out=sc_ya[:, cc, :],
                                        in_=sc_hC[:].rearrange("p n t -> p t n"),
                                        axis=mybir.AxisListType.X, op=ALU.add
                                        ).then_inc(eV, 1)
                    else:
                        V.tensor_reduce(out=sc_yt[:],
                                        in_=sc_hC[:].rearrange("p n t -> p t n"),
                                        axis=mybir.AxisListType.X, op=ALU.add
                                        ).then_inc(eV, 1)
                        V.tensor_add(sc_ya[:, cc, :], sc_ya[:, cc, :], sc_yt[:]
                                     ).then_inc(eV, 1)
                rep_free[qsl] = (eV, eV.n)
                if nq < 2:
                    nq_tgt[nq + 2] = issue_bcast(nq + 2, rep_free[qsl])
                nu = len(unit_marks)
                u = min(nu, int(np.ceil(nu * (b + (nq + 1) / 4.0) / 9.0)))
                if u > 0:
                    mark = unit_marks[u - 1]
                    splice_to(S, S2, s2_state, mark[0])
                    splice_to(V, V2, v2_state, mark[1])
                    splice_to(Y, Y2, y2_state, mark[2])
            for cc in range(4):
                V.scalar_tensor_tensor(out=sc_yp[:], in0=sc_u[sl][:, cc, :],
                                       scalar=dprm_sb[:, cc:cc + 1],
                                       in1=sc_ya[:, cc, :], op0=ALU.mult,
                                       op1=ALU.add).then_inc(eV, 1)
                S.wait(*ga_ld)
                if tg_rd[0] is not None:
                    S.wait(*tg_rd)
                S.activation(out=sc_tg[:], in_=sc_ga[:, cc, :], func=AF.Tanh,
                             scale=0.5).then_inc(eS, 1)
                V.wait(eS)
                V.scalar_tensor_tensor(out=sc_gr[:], in0=sc_tg[:], scalar=1.0,
                                       in1=sc_ga[:, cc, :], op0=ALU.add,
                                       op1=ALU.mult).then_inc(eV, 1)
                tg_rd = (eV, eV.n)
                if yf_prev[cc % 2][0] is not None:
                    V.wait(*yf_prev[cc % 2])
                V.tensor_mul(sc_yf[cc % 2][:], sc_yp[:], sc_gr[:]).then_inc(eV, 1)
                yf_prev[cc % 2] = st_yf.issue(
                    Y, cc % 2,
                    g_y_in[b // 2, :, cc, (b % 2) * TB:(b % 2) * TB + TB],
                    sc_yf[cc % 2][:], gate=(eV, eV.n))
            de_free[sl] = (eV, eV.n)
            ga_free = (eV, eV.n)
        splice_to(S, S2, s2_state, len(S2.calls))
        splice_to(V, V2, v2_state, len(V2.calls))
        splice_to(Y, Y2, y2_state, len(Y2.calls))
        G.wait(*st_yf.tgt(0))
        G.wait(*st_yf.tgt(1))
        with allow_ici_all_to_all():
            G.collective_compute("AllToAll", ALU.bypass, replica_groups=RG,
                                 ins=[g_y_in], outs=[g_y_out]).then_inc(qC, 1)
        a2a_y = qC.n

        Y.wait(qC, a2a_y)
        for r in range(NCORE):
            ysb_ld = st_ysb.issue(Y, 0, y_sb[:, r * 4:(r + 1) * 4, :],
                                  g_y_out[r, :, :, :])
        opw_free = [(None, 0), (None, 0)]
        g0_free = [(None, 0), (None, 0)]
        xr_free = [(None, 0), (None, 0)]
        T.wait(*ysb_ld)
        for o in range(16):
            sl = o % 2
            wl = sg_op.issue(G, sl, opwb[sl][:], opw[:, o * 128:(o + 1) * 128]
                             .rearrange("(dc p) j -> p dc j", p=128),
                             gate=opw_free[sl])
            if o == 0:
                Y.wait(*st_g0.tgt(0))
                Y.wait(*st_g0.tgt(1))
            st_gl.issue(Y, sl, g0ld[sl][:], sp_g0[o, :, :], gate=g0_free[sl])
            ld = st_xr.issue(Y, sl, xrld[sl][:], x_t[o * 128:(o + 1) * 128, :],
                             gate=xr_free[sl])
            gld = st_gl.tgt(sl)
            bks = [0, 1] if sl == 0 else [2, 3]
            T.wait(*wl)
            bank_wait(T, bks)
            for dc in range(32):
                for th in range(2):
                    mm = T.matmul(out=pb(bks[0] + th),
                                  lhsT=opwb[sl][:, dc, :],
                                  rhs=y_sb[:, dc, th * 512:(th + 1) * 512],
                                  start=(dc == 0), stop=(dc == 31))
            mm.then_inc(eT, 1)
            mt = eT.n
            opw_free[sl] = (eT, mt)
            V.wait(eT, mt)
            V.wait(*ld)
            V.wait(*gld)
            if o >= 1:
                V.wait(*st_x1.tgt(0))
            V.tensor_mul(vtmp[:], pb2(bks[0]), g0ld[sl][:]).then_inc(eV, 1)
            V.scalar_tensor_tensor(out=x1st[:], in0=vtmp[:], scalar=0.25,
                                   in1=xrld[sl][:], op0=ALU.mult, op1=ALU.add
                                   ).then_inc(eV, 1)
            x1v = eV.n
            bank_used(bks, eV, x1v)
            g0_free[sl] = (eV, x1v)
            xr_free[sl] = (eV, x1v)
            st_x1.issue(Y, 0, sp_x1[o, :, :], x1st[:], gate=(eV, x1v))
            S.wait(eV, x1v)
            S.copy(out=x1b[:, o, :], in_=x1st[:]).then_inc(eS, 1)
            S.activation(out=xsqb[:], in_=x1b[:, o, :], func=AF.Square
                         ).then_inc(eS, 1)
            sq = eS.n
            T.wait(eS, sq)
            if o == 0:
                bank_wait(T, [4, 5, 6, 7])
            for th in range(2):
                T.matmul(out=pb(4 + th), lhsT=onesb[:],
                         rhs=x1b[:, o, th * 512:(th + 1) * 512],
                         start=(o == 0), stop=(o == 15))
                mm = T.matmul(out=pb(6 + th), lhsT=onesb[:],
                              rhs=xsqb[:, th * 512:(th + 1) * 512],
                              start=(o == 0), stop=(o == 15))
            mm.then_inc(eT, 1)
        V.wait(eT)
        V.tensor_scalar_mul(mean_rep[:], pb2(4), 1.0 / HID).then_inc(eV, 1)
        V.tensor_scalar_mul(e2_rep[:], pb2(6), 1.0 / HID).then_inc(eV, 1)
        V.tensor_mul(vtmp[:], mean_rep[:], mean_rep[:]).then_inc(eV, 1)
        V.tensor_sub(vtmp[:], e2_rep[:], vtmp[:]).then_inc(eV, 1)
        bank_used([4, 5, 6, 7], eV, eV.n)
        S.wait(eV)
        S.activation(out=vtmp[:], in_=vtmp[:], func=AF.Ln, bias=eps_sb[:, 0:1]).then_inc(eS, 1)
        S.activation(out=rstd_rep[:], in_=vtmp[:], func=AF.Exp, scale=-0.5
                     ).then_inc(eS, 1)
        rstd1 = eS.n
        for hc in range(16):
            V.wait(eS, rstd1)
            V.tensor_sub(vtmp[:], x1b[:, hc, :], mean_rep[:]).then_inc(eV, 1)
            V.tensor_mul(xn_buf[:, hc, :], vtmp[:], rstd_rep[:]).then_inc(eV, 1)
        xn1_done = eV.n

        wqk_free = [(None, 0), (None, 0)]
        T.wait(eV, xn1_done)
        G.wait(eV, xn1_done)
        S.wait(*st_x1.tgt(0))
        for oc in range(16):
            sl = oc % 2
            wl = sg_q.issue(G, sl, wqkb[sl][:], wqk[:, oc * 128:(oc + 1) * 128]
                            .rearrange("(kc p) j -> p kc j", p=128),
                            gate=wqk_free[sl])
            bks = [0, 1] if sl == 0 else [2, 3]
            T.wait(*wl)
            bank_wait(T, bks)
            for kc in range(16):
                for th in range(2):
                    mm = T.matmul(out=pb(bks[0] + th),
                                  lhsT=wqkb[sl][:, kc, :],
                                  rhs=xn_buf[:, kc, th * 512:(th + 1) * 512],
                                  start=(kc == 0), stop=(kc == 15))
            mm.then_inc(eT, 1)
            mt = eT.n
            wqk_free[sl] = (eT, mt)
            V.wait(eT, mt)
            if oc >= 2:
                V.wait(eS)
            V.tensor_scalar(out=elu_m[:], in0=pb2(bks[0]),
                            scalar1=bqk_sb[:, oc:oc + 1], scalar2=0.0,
                            op0=ALU.add, op1=ALU.min).then_inc(eV, 1)
            V.tensor_scalar(out=elu_e[:], in0=pb2(bks[0]),
                            scalar1=bqk_sb[:, oc:oc + 1], scalar2=0.0,
                            op0=ALU.add, op1=ALU.max).then_inc(eV, 1)
            bank_used(bks, eV, eV.n)
            S.wait(eV)
            S.activation(out=elu_m[:], in_=elu_m[:], func=AF.Exp).then_inc(eS, 1)
            V.wait(eS)
            V.scalar_tensor_tensor(out=qf32[sl][:], in0=elu_m[:], scalar=-1.0,
                                   in1=elu_e[:], op0=ALU.add, op1=ALU.add
                                   ).then_inc(eV, 1)
            qv = eV.n
            S.wait(eV, qv)
            S.copy(out=qfm[:, oc, :], in_=qf32[sl][:]).then_inc(eS, 1)
            T.wait(eV, qv)
            for tc in range(8):
                tb_ = 4 + 2 * sl + (tc % 2)
                bank_wait(T, [tb_])
                T.transpose(out=pb(tb_, 128), in_=qf32[sl][:, tc * 128:(tc + 1) * 128],
                            identity=idf_sb[:]).then_inc(eT, 1)
                S.wait(eT)
                S.copy(out=qtok[:, tc, oc, :], in_=pb(tb_, 128)).then_inc(eS, 1)
                bank_used([tb_], eS, eS.n)
        wv_free = list(wqk_free)
        wvb = wqkb
        for hg in range(16):
            sl = hg % 2
            wl = sg_q.issue(G, sl, wvb[sl][:], wv[:, hg * 128:(hg + 1) * 128]
                            .rearrange("(kc p) j -> p kc j", p=128),
                            gate=wv_free[sl])
            T.wait(*wl)
            for tc in range(8):
                bk = (hg * 8 + tc) % 4
                bank_wait(T, [bk])
                for kc in range(16):
                    mm = T.matmul(out=pb(bk, 128),
                                  lhsT=xn_buf[:, kc, tc * 128:(tc + 1) * 128],
                                  rhs=wvb[sl][:, kc, :],
                                  start=(kc == 0), stop=False)
                mm = T.matmul(out=pb(bk, 128), lhsT=ones1b[:],
                              rhs=bvr_sb[:, hg * 128:(hg + 1) * 128],
                              start=False, stop=True)
                mm.then_inc(eT, 1)
                mt = eT.n
                S.wait(eT, mt)
                if hg + tc > 0:
                    S.wait(eV)
                S.activation(out=tstg[0][:], in_=pb(bk, 128), func=AF.Tanh,
                             scale=0.5).then_inc(eS, 1)
                V.wait(eS)
                V.scalar_tensor_tensor(out=vtok[:, tc, hg * 128:(hg + 1) * 128],
                                       in0=tstg[0][:], scalar=1.0,
                                       in1=pb(bk, 128), op0=ALU.add, op1=ALU.mult
                                       ).then_inc(eV, 1)
                bank_used([bk], eV, eV.n)
            wv_free[sl] = (eT, eT.n)
        v_done = eV.n
        T.wait(eV, v_done)
        for hb in range(4):
            bank_wait(T, [4, 5, 6, 7])
            for tc in range(8):
                for hh in range(4):
                    h = hb * 4 + hh
                    mm = T.matmul(out=pb(4 + hh, 128),
                                  lhsT=qtok[:, tc, h, :],
                                  rhs=vtok[:, tc, h * 128:(h + 1) * 128],
                                  start=(tc == 0), stop=(tc == 7))
            mm.then_inc(eT, 1)
            mt = eT.n
            S.wait(eT, mt)
            for hh in range(4):
                h = hb * 4 + hh
                S.copy(out=kv_sb[:, h * 128:(h + 1) * 128], in_=pb(4 + hh, 128)
                       ).then_inc(eS, 1)
            bank_used([4, 5, 6, 7], eS, eS.n)
        kv_st = st_kv.issue(Y, 0, g_kv_in[:], kv_sb[:], gate=(eS, eS.n))
        G.wait(*kv_st)
        G.collective_compute("AllReduce", ALU.add, replica_groups=RG,
                             ins=[g_kv_in], outs=[g_kv_out]).then_inc(qC, 1)
        ar_kv = qC.n

        wu_free = list(wv_free)
        for oc in range(16):
            sl = oc % 2
            wl = sg_q.issue(G, sl, wqkb[sl][:], wu[:, oc * 128:(oc + 1) * 128]
                            .rearrange("(kc p) j -> p kc j", p=128),
                            gate=wu_free[sl])
            bks = [0, 1] if sl == 0 else [2, 3]
            T.wait(*wl)
            bank_wait(T, bks)
            for kc in range(16):
                for th in range(2):
                    mm = T.matmul(out=pb(bks[0] + th),
                                  lhsT=wqkb[sl][:, kc, :],
                                  rhs=xn_buf[:, kc, th * 512:(th + 1) * 512],
                                  start=(kc == 0), stop=(kc == 15))
            mm.then_inc(eT, 1)
            mt = eT.n
            wu_free[sl] = (eT, mt)
            S.wait(eT, mt)
            if oc > 0:
                S.wait(eV)
            S.activation(out=m_t1[:], in_=pb2(bks[0]), func=AF.Tanh, scale=0.5,
                         bias=buh_sb[:, oc:oc + 1]).then_inc(eS, 1)
            ev = eS.n
            V.wait(eS, ev)
            V.tensor_scalar(out=vtmp[:], in0=pb2(bks[0]),
                            scalar1=bu_sb[:, oc:oc + 1], scalar2=None,
                            op0=ALU.add).then_inc(eV, 1)
            if oc >= 1:
                V.wait(*st_u1.tgt(0))
            V.scalar_tensor_tensor(out=u1st[:], in0=m_t1[:], scalar=1.0,
                                   in1=vtmp[:], op0=ALU.add, op1=ALU.mult
                                   ).then_inc(eV, 1)
            uv = eV.n
            bank_used(bks, eV, uv)
            st_u1.issue(Y, 0, sp_u1[oc, :, :], u1st[:], gate=(eV, uv))
        T.wait(*smalls)
        for tc in range(8):
            bk = tc % 4
            bank_wait(T, [bk])
            for kc in range(16):
                mm = T.matmul(out=pBig[:, bk * 512:bk * 512 + NE],
                              lhsT=xn_buf[:, kc, tc * 128:(tc + 1) * 128],
                              rhs=rwb[:, kc, :], start=(kc == 0), stop=(kc == 15))
            mm.then_inc(eT, 1)
            mt = eT.n
            V.wait(eT, mt)
            V.tensor_copy(out=sco[:], in_=pBig[:, bk * 512:bk * 512 + NE]
                          ).then_inc(eV, 1)
            V.op("max", out=sst[:], in_=sco[:]).then_inc(eV, 1)
            V.tensor_scalar(out=swt[:], in0=sco[:], scalar1=sst[:, 0:1],
                            scalar2=None, op0=ALU.subtract).then_inc(eV, 1)
            V.tensor_scalar(out=sd1[:], in0=sst[:, 1:2], scalar1=sst[:, 0:1],
                            scalar2=None, op0=ALU.subtract).then_inc(eV, 1)
            bank_used([bk], eV, eV.n)
            S.wait(eV)
            S.activation(out=swt[:], in_=swt[:], func=AF.Exp).then_inc(eS, 1)
            S.activation(out=sd1[:], in_=sd1[:], func=AF.Exp).then_inc(eS, 1)
            sx = eS.n
            V.wait(eS, sx)
            V.tensor_scalar(out=sd2[:], in0=sd1[:], scalar1=1.0, scalar2=2.0,
                            op0=ALU.add, op1=ALU.mult).then_inc(eV, 1)
            V.reciprocal(out=sd2[:], in_=sd2[:]).then_inc(eV, 1)
            V.tensor_scalar(out=sco[:], in0=sco[:], scalar1=sst[:, 1:2],
                            scalar2=None, op0=ALU.is_ge).then_inc(eV, 1)
            V.tensor_mul(swt[:], swt[:], sco[:]).then_inc(eV, 1)
            V.tensor_scalar(out=swt[:], in0=swt[:], scalar1=sd2[:, 0:1],
                            scalar2=None, op0=ALU.mult).then_inc(eV, 1)
            wv_ = eV.n
            bank_wait(T, [4])
            T.wait(eV, wv_)
            T.transpose(out=pBig[0:NE, 4 * 512:4 * 512 + 128], in_=swt[:],
                        identity=idf_sb[:]).then_inc(eT, 1)
            S.wait(eT)
            S.copy(out=wtf[:, tc * 128:(tc + 1) * 128],
                   in_=pBig[0:NE, 4 * 512:4 * 512 + 128]).then_inc(eS, 1)
            bank_used([4], eS, eS.n)
        wts_dma = sg_wt.issue(G, 0, g_wts[:], wtf[:], gate=(eS, eS.n))
        for e in range(NE):
            wrep_ld = st_wr.issue(Y, 0, wrep[:, e, :], bcast(g_wts[e:e + 1, :]),
                                  gate=wts_dma if e == 0 else None)
        w13_free = [wu_free[0], wu_free[1], wu_free[0], wu_free[1]]
        for e in range(NE):
            for fc in range(4):
                s1i, s3i = (fc % 2), 2 + (fc % 2)
                wls = []
                for wm, sbuf_i in [(w1, s1i), (w3, s3i)]:
                    wls.append(sg_13.issue(
                        G, sbuf_i, w13b[sbuf_i][:],
                        wm[e, :, fc * 128:(fc + 1) * 128]
                        .rearrange("(kc p) j -> p kc j", p=128),
                        gate=w13_free[sbuf_i]))
                bks = [0, 1, 2, 3] if fc % 2 == 0 else [4, 5, 6, 7]
                T.wait(*wls[0])
                T.wait(*wls[1])
                bank_wait(T, bks)
                for kc in range(16):
                    for th in range(2):
                        T.matmul(out=pb(bks[0] + th), lhsT=w13b[s1i][:, kc, :],
                                 rhs=xn_buf[:, kc, th * 512:(th + 1) * 512],
                                 start=(kc == 0), stop=(kc == 15))
                for kc in range(16):
                    for th in range(2):
                        mm = T.matmul(out=pb(bks[2] + th), lhsT=w13b[s3i][:, kc, :],
                                      rhs=xn_buf[:, kc, th * 512:(th + 1) * 512],
                                      start=(kc == 0), stop=(kc == 15))
                mm.then_inc(eT, 1)
                mt = eT.n
                w13_free[s1i] = (eT, mt)
                w13_free[s3i] = (eT, mt)
                S.wait(eT, mt)
                if e + fc > 0:
                    S.wait(eV)
                S.activation(out=m_t1[:], in_=pb2(bks[0]), func=AF.Tanh,
                             scale=0.5).then_inc(eS, 1)
                ev = eS.n
                V.wait(eS, ev)
                V.wait(*wrep_ld)
                V.scalar_tensor_tensor(out=m_g1s[:], in0=m_t1[:], scalar=1.0,
                                       in1=pb2(bks[0]), op0=ALU.add, op1=ALU.mult
                                       ).then_inc(eV, 1)
                V.tensor_mul(m_g1s[:], m_g1s[:], pb2(bks[2])).then_inc(eV, 1)
                V.tensor_mul(he_sb[:, e * 4 + fc, :], m_g1s[:], wrep[:, e, :]
                             ).then_inc(eV, 1)
                bank_used(bks, eV, eV.n)
        he_done = eV.n
        w2_free = [(None, 0), (None, 0)]
        T.wait(eV, he_done)
        for o in range(16):
            bks = [0, 1] if o % 2 == 0 else [2, 3]
            bank_wait(T, bks)
            first = True
            for e in range(NE):
                sl = e % 2
                wl = sg_w2.issue(G, sl, w2b[sl][:],
                                 w2[e, :, o * 128:(o + 1) * 128]
                                 .rearrange("(fc p) j -> p fc j", p=128),
                                 gate=w2_free[sl])
                T.wait(*wl)
                for fc in range(4):
                    for th in range(2):
                        mm = T.matmul(out=pb(bks[0] + th), lhsT=w2b[sl][:, fc, :],
                                      rhs=he_sb[:, e * 4 + fc, th * 512:(th + 1) * 512],
                                      start=first, stop=(e == NE - 1 and fc == 3))
                    first = False
                mm.then_inc(eT, 1)
                w2_free[sl] = (eT, eT.n)
            mt = eT.n
            S.wait(eT, mt)
            if o > 0:
                S.wait(eV)
            S.activation(out=m_t1[:], in_=pb2(bks[0]), func=AF.Tanh, scale=0.5
                         ).then_inc(eS, 1)
            V.wait(eS)
            V.scalar_tensor_tensor(out=xn_buf[:, o, :], in0=m_t1[:], scalar=1.0,
                                   in1=pb2(bks[0]), op0=ALU.add, op1=ALU.mult
                                   ).then_inc(eV, 1)
            bank_used(bks, eV, eV.n)
        g1_done = eV.n

        G.wait(qC, ar_kv)
        kv_ld = sg_kv.issue(G, 0, kvbf[:],
                            g_kv_out[:].rearrange("p (h j) -> p h j", j=128))
        T.wait(*kv_ld)
        for h in range(16):
            bks = [0, 1] if h % 2 == 0 else [2, 3]
            bank_wait(T, bks)
            for th in range(2):
                mm = T.matmul(out=pb(bks[0] + th), lhsT=kvbf[:, h, :],
                              rhs=qfm[:, h, th * 512:(th + 1) * 512],
                              start=True, stop=True)
            mm.then_inc(eT, 1)
            mt = eT.n
            S.wait(eT, mt)
            S.copy(out=ao_sb[:, h, :], in_=pb2(bks[0])).then_inc(eS, 1)
            S.activation(out=xsqb[:], in_=ao_sb[:, h, :], func=AF.Square
                         ).then_inc(eS, 1)
            sq = eS.n
            bank_used(bks, eS, sq)
            T.wait(eS, sq)
            if h == 0:
                bank_wait(T, [4, 5, 6, 7])
            for th in range(2):
                T.matmul(out=pb(4 + th), lhsT=onesb[:],
                         rhs=ao_sb[:, h, th * 512:(th + 1) * 512],
                         start=(h == 0), stop=(h == 15))
                mm = T.matmul(out=pb(6 + th), lhsT=onesb[:],
                              rhs=xsqb[:, th * 512:(th + 1) * 512],
                              start=(h == 0), stop=(h == 15))
            mm.then_inc(eT, 1)
        V.wait(eT)
        V.tensor_scalar_mul(mean_rep[:], pb2(4), 1.0 / HID).then_inc(eV, 1)
        V.tensor_scalar_mul(e2_rep[:], pb2(6), 1.0 / HID).then_inc(eV, 1)
        V.tensor_mul(vtmp[:], mean_rep[:], mean_rep[:]).then_inc(eV, 1)
        V.tensor_sub(vtmp[:], e2_rep[:], vtmp[:]).then_inc(eV, 1)
        bank_used([4, 5, 6, 7], eV, eV.n)
        S.wait(eV)
        S.activation(out=vtmp[:], in_=vtmp[:], func=AF.Ln, bias=eps_sb[:, 0:1]).then_inc(eS, 1)
        S.activation(out=rstd_rep[:], in_=vtmp[:], func=AF.Exp, scale=-0.5
                     ).then_inc(eS, 1)
        rstd2 = eS.n
        fin7 = eV.n
        u1_free = [(None, 0), (None, 0)]
        Y.wait(*st_u1.tgt(0))
        Y.wait(eV, fin7)
        Y.wait(eV, g1_done)
        Y.wait(qC, ar_kv)
        for hc in range(16):
            sl = hc % 2
            ld = st_ul.issue(Y, sl, u1ld[sl][:], sp_u1[hc, :, :],
                             gate=u1_free[sl])
            V.wait(eS, rstd2)
            V.wait(*ld)
            V.tensor_sub(nrst[:], ao_sb[:, hc, :], mean_rep[:]).then_inc(eV, 1)
            V.tensor_mul(nrst[:], nrst[:], rstd_rep[:]).then_inc(eV, 1)
            V.tensor_scalar(out=g2st[:], in0=nrst[:], scalar1=lng_sb[:, hc:hc + 1],
                            scalar2=lnb_sb[:, hc:hc + 1], op0=ALU.mult, op1=ALU.add
                            ).then_inc(eV, 1)
            V.tensor_mul(gated[:, hc, :], g2st[:], u1ld[sl][:]).then_inc(eV, 1)
            u1_free[sl] = (eV, eV.n)
        gated_done = eV.n
        wo_free = [(None, 0), (None, 0)]
        T.wait(eV, gated_done)
        for o in range(16):
            sl = o % 2
            wl = sg_wo.issue(G, sl, wob[sl][:], wo[:, o * 128:(o + 1) * 128]
                             .rearrange("(kc p) j -> p kc j", p=128),
                             gate=wo_free[sl])
            bks = [0, 1] if sl == 0 else [2, 3]
            T.wait(*wl)
            bank_wait(T, bks)
            for kc in range(16):
                for th in range(2):
                    mm = T.matmul(out=pb(bks[0] + th), lhsT=wob[sl][:, kc, :],
                                  rhs=gated[:, kc, th * 512:(th + 1) * 512],
                                  start=(kc == 0), stop=(kc == 15))
            mm.then_inc(eT, 1)
            mt = eT.n
            wo_free[sl] = (eT, mt)
            V.wait(eT, mt)
            V.tensor_scalar(out=s1_sb[:, o, :], in0=pb2(bks[0]), scalar1=0.5,
                            scalar2=bo_sb[:, o:o + 1], op0=ALU.mult, op1=ALU.add
                            ).then_inc(eV, 1)
            bank_used(bks, eV, eV.n)
        s1_done = eV.n

        x1l_free = [(None, 0), (None, 0)]
        Y.wait(*st_x1.tgt(0))
        Y.wait(eV, s1_done)
        for hc in range(16):
            sl = hc % 2
            ld = st_xl.issue(Y, sl, x1l[sl][:], sp_x1[hc, :, :],
                             gate=x1l_free[sl])
            V.wait(*ld)
            if hc >= 2:
                V.wait(*st_o.tgt(sl))
            V.tensor_mul(outst[sl][:], s1_sb[:, hc, :], xn_buf[:, hc, :]
                         ).then_inc(eV, 1)
            V.scalar_tensor_tensor(out=outst[sl][:], in0=outst[sl][:], scalar=0.5,
                                   in1=x1l[sl][:], op0=ALU.mult, op1=ALU.add
                                   ).then_inc(eV, 1)
            ov = eV.n
            x1l_free[sl] = (eV, ov)
            st_o.issue(Y, sl, out_t[hc * 128:(hc + 1) * 128, :], outst[sl][:],
                       gate=(eV, ov))
        Y.wait(*st_o.tgt(0))
        Y.wait(*st_o.tgt(1))

        with allow_ici_all_to_all(), nc.Block() as block:
            @block.tensor
            def _(t):
                T.replay(t)

            @block.vector
            def _(v):
                V.replay(v, auto_drain=True)

            @block.scalar
            def _(s):
                S.replay(s, auto_drain=True)

            @block.gpsimd
            def _(g):
                G.replay(g, auto_drain=True)

            @block.sync
            def _(sy):
                Y.replay(sy)

    return nc


_NC_CACHE = None


def kernel(**inp):
    global _NC_CACHE
    f32 = np.float32

    def C(a):
        return np.ascontiguousarray(a)

    x = np.asarray(inp["L0_attn_hidden_states"], f32)[0]
    xT = C(x.T)
    ipwT = C(np.asarray(inp["L0_attn_in_proj_w"], f32).T)
    convw = np.asarray(inp["L0_attn_conv_w"], f32)[:, 0, :]
    convb = np.asarray(inp["L0_attn_conv_b"], f32)
    alog = np.asarray(inp["L0_attn_A_log"], f32)
    dparam = np.asarray(inp["L0_attn_D_param"], f32)
    dtb = np.asarray(inp["L0_attn_dt_proj_b"], f32)
    xpwT = C(np.asarray(inp["L0_attn_x_proj_w"], f32).T)
    dtwT = C(np.asarray(inp["L0_attn_dt_proj_w"], f32).T)
    opwT = C(np.asarray(inp["L0_attn_out_proj_w"], f32).T)
    gwT = C(np.asarray(inp["L0_ffn_gate_w"], np.int32).T)
    uwT = C(np.asarray(inp["L0_ffn_up_w"], np.int32).T)
    dwnT = C(np.asarray(inp["L0_ffn_down_w"], np.int32).T)
    gscv = np.asarray(inp["L0_ffn_gate_scale"], f32)
    uscv = np.asarray(inp["L0_ffn_up_scale"], f32)
    dscv = np.asarray(inp["L0_ffn_down_scale"], f32)
    rwT = C(np.asarray(inp["L1_ffn_router_w"], f32).T)

    def col(v, nchunk):
        return C(v.reshape(nchunk, 128).T)

    shards = []
    for c in range(NCORE):
        tsl = slice(c * TOK, (c + 1) * TOK)
        csl = slice(c * DL, (c + 1) * DL)

        def ch4(m):
            return C(m.reshape(4, 128, *m.shape[1:]).transpose(
                1, 0, *range(2, m.ndim + 1)))

        d = {
            "x_t": C(xT[:, tsl]),
            "ipw": C(np.concatenate([ipwT[:, csl],
                                     ipwT[:, 4096 + c * DL:4096 + (c + 1) * DL]], 1)),
            "convw": ch4(convw[csl]),
            "convb": C(convb[csl].reshape(4, 128).T),
            "alog": ch4(alog[csl]),
            "dparam": C(dparam[csl].reshape(4, 128).T),
            "dtbi": C(dtb[csl].reshape(4, 128).T),
            "xpw": ch4(xpwT[csl]),
            "dtw": C(dtwT[:, csl]),
            "opw": opwT,
            "gw": gwT, "uw": uwT, "dwn": dwnT,
            "gsc": col(gscv, 64), "usc": col(uscv, 64), "dsc": col(dscv, 16),
            "wqk": np.asarray(inp["L1_attn_Wqk"], f32),
            "wv": np.asarray(inp["L1_attn_Wv"], f32),
            "wu": np.asarray(inp["L1_attn_Wu"], f32),
            "wo": np.asarray(inp["L1_attn_Wo"], f32),
            "bqk": col(np.asarray(inp["L1_attn_bqk"], f32), 16),
            "bvr": np.asarray(inp["L1_attn_bv"], f32).reshape(1, HID),
            "bui": col(np.asarray(inp["L1_attn_bu"], f32), 16),
            "boi": col(np.asarray(inp["L1_attn_bo"], f32), 16),
            "lng": col(np.asarray(inp["L1_attn_ln_g"], f32), 16),
            "lnb": col(np.asarray(inp["L1_attn_ln_b"], f32), 16),
            "rw": rwT,
            "w1": np.asarray(inp["L1_ffn_w1"], f32),
            "w2": np.asarray(inp["L1_ffn_w2"], f32),
            "w3": np.asarray(inp["L1_ffn_w3"], f32),
            "idf": np.eye(128, dtype=f32),
        }
        shards.append(d)

    nc = build_nc()
    res = run_bass_kernel_spmd(nc, shards, core_ids=list(range(NCORE)))
    out = np.concatenate([r["out_t"] for r in res.results], axis=1)
    return C(out.T).reshape(1, GT, HID)


if __name__ == "__main__":
    n = build_nc()
    print("built ok")
